# version 01 (final)
# baseline (speedup 1.0000x reference)
import numpy as np
import ml_dtypes
import concourse.bass as bass
import concourse.mybir as mybir
from concourse.bass_utils import run_bass_kernel_spmd
from concourse.bass_core_rust import IndirectOffsetOnAxis

F32 = mybir.dt.float32
BF16 = mybir.dt.bfloat16
I32 = mybir.dt.int32
AF = mybir.ActivationFunctionType
ALU = mybir.AluOpType
AX = mybir.AxisListType
BF16NP = ml_dtypes.bfloat16
ds = bass.ds

B, S, D = 4, 4096, 1024
NCORES = 8
T = S // 2
H, DK = 16, 64
C0 = 64
MRP = 128
DH = 4096
LH, LDH = 8, 64
INNER = LH * LDH
WIN = 512
HID = 2730
HIDP = 2816
NEG = -1.0e30
DCH = D // 128
TB = 512
NTB = T // TB
NQB = S // TB
EPS = 1e-5
REPL = [[0, 1], [2, 3], [4, 5], [6, 7]]


def _f32(x):
    return np.ascontiguousarray(np.asarray(x, dtype=np.float32))



def prepare_host(inputs):
    g = {}
    for n in ("Wq", "Wk", "Wv", "Wo"):
        g[f"l0_{n}"] = _f32(inputs[f"L0_attn_{n}"])
    relb = _f32(inputs["L0_attn_rel_bias"])
    relb2 = np.zeros((65, 8 * 257), np.float32)
    for hh in range(H):
        r, c = (hh % 2) * 64, hh // 2
        relb2[r, c * 257:(c + 1) * 257] = relb[:, hh]
    g["l0_relb2"] = relb2

    kk = np.arange(128)[:, None]
    qq = np.arange(128)[None, :]
    d_diag = np.clip(qq - kk, -MRP, MRP) + MRP
    d_prev = np.clip(qq + 128 - kk, -MRP, MRP) + MRP
    mask_diag = (qq // C0) < (kk // C0)
    pat_diag = np.zeros((H, 128, 128), np.float32)
    pat_prev = np.zeros((H, 128, 128), np.float32)
    for h in range(H):
        col = relb[:, h] - relb[256, h]
        pd = col[d_diag].copy()
        pd[mask_diag] = NEG
        pat_diag[h] = pd
        pat_prev[h] = col[d_prev]
    g["_pat_diag"] = pat_diag
    g["_pat_prev"] = pat_prev

    g["l0_w1T"] = _f32(np.asarray(inputs["L0_ffn_w1"]).T)
    g["l0_w2T"] = _f32(np.asarray(inputs["L0_ffn_w2"]).T)
    cw = _f32(np.asarray(inputs["L0_ffn_conv_w"]).reshape(DH, 3))
    cwc = np.zeros((128, 96), np.float32)
    for m in range(32):
        for tap in range(3):
            cwc[:, m * 3 + tap] = cw[m * 128:(m + 1) * 128, tap]
    g["l0_convw_cols"] = cwc

    for n in ("Wq", "Wk", "Wv", "Wo"):
        g[f"l1a_{n}"] = _f32(np.asarray(inputs[f"L1_attn_{n}"]).T)

    for l in (0, 1):
        g[f"l1f_wqkv{l}"] = _f32(inputs["L1_ffn_wqkv"][l])
        g[f"l1f_wo{l}"] = _f32(inputs["L1_ffn_wo"][l])
        w1 = _f32(inputs["L1_ffn_w1"][l])
        w1a = np.zeros((D, HIDP), np.float32); w1a[:, :HID] = w1[:, :HID]
        w1g = np.zeros((D, HIDP), np.float32); w1g[:, :HID] = w1[:, HID:]
        g[f"l1f_w1a{l}"] = _f32(w1a)
        g[f"l1f_w1g{l}"] = _f32(w1g)
        w2p = np.zeros((HIDP, D), np.float32)
        w2p[:HID] = _f32(inputs["L1_ffn_w2"][l])
        g[f"l1f_w2{l}"] = w2p

    g["l0_bv_row"] = _f32(inputs["L0_attn_bv"]).reshape(1, D)
    g["l1a_bv_row"] = _f32(inputs["L1_attn_bv"]).reshape(1, D)

    bias_specs = [
        ("l0_bq", inputs["L0_attn_bq"]), ("l0_bk", inputs["L0_attn_bk"]),
        ("l0_bo", inputs["L0_attn_bo"]),
        ("l0_convb", np.asarray(inputs["L0_ffn_conv_b"])),
        ("l1a_bq", inputs["L1_attn_bq"]), ("l1a_bk", inputs["L1_attn_bk"]),
        ("l1a_bo", inputs["L1_attn_bo"]),
    ]
    for l in (0, 1):
        b1 = _f32(inputs["L1_ffn_b1"][l])
        b1a = np.zeros((HIDP,), np.float32); b1a[:HID] = b1[:HID]
        b1g = np.zeros((HIDP,), np.float32); b1g[:HID] = b1[HID:]
        bias_specs += [
            (f"l1f_b1a{l}", b1a), (f"l1f_b1g{l}", b1g),
            (f"l1f_b2{l}", inputs["L1_ffn_b2"][l]),
            (f"l1f_lng{l}", inputs["L1_ffn_ln_g"][l]),
            (f"l1f_lnb{l}", inputs["L1_ffn_ln_b"][l]),
            (f"l1f_qs{l}", np.concatenate([_f32(inputs["L1_ffn_q_scale"][l])] * 2)),
            (f"l1f_ks{l}", np.concatenate([_f32(inputs["L1_ffn_k_scale"][l])] * 2)),
        ]
    cols, bidx = [], {}
    for name, vec in bias_specs:
        v = _f32(vec).ravel()
        assert v.size % 128 == 0, name
        bidx[name] = len(cols)
        for c in range(v.size // 128):
            cols.append(v[c * 128:(c + 1) * 128])
    g["biases"] = _f32(np.stack(cols, axis=1))
    NB = g["biases"].shape[1]

    g["ident_bf"] = np.eye(128, dtype=BF16NP)
    g["ident_f"] = np.eye(128, dtype=np.float32)
    g["onesrow_bf"] = np.ones((1, 128), dtype=BF16NP)
    g["onesrow_f"] = np.ones((1, 128), np.float32)
    g["onesall_f"] = np.ones((128, 128), np.float32)
    g["onescol_bf"] = np.ones((128, 1), dtype=BF16NP)
    ind2 = np.zeros((128, 65), dtype=BF16NP); ind2[:64, 0] = 1; ind2[64:, 64] = 1
    g["ind2_bf"] = ind2
    g["ones2_f"] = np.ones((2, 128), np.float32)
    m216 = np.zeros((2, 16), np.float32)
    m216[0, 0::2] = 1; m216[1, 1::2] = 1
    g["mask216_f"] = m216
    perm = np.zeros((128, 128), np.float32)
    for hh in range(2):
        o = hh * 64
        for dd in range(32):
            perm[o + dd + 32, o + dd] = -1.0
            perm[o + dd, o + dd + 32] = 1.0
    g["perm_bf"] = perm.astype(BF16NP)
    tri0 = np.where(qq <= kk, 0.0, NEG).astype(np.float32)
    tri1 = np.where(qq >= kk, 0.0, NEG).astype(np.float32)
    zz = np.zeros((128, 128), np.float32)
    g["loc_masks"] = _f32(np.concatenate([tri0, tri1, zz], axis=1))
    mw = np.zeros((8, 128, 512), np.float32)
    for kt in range(8):
        for qs_ in range(4):
            blk = mw[kt, :, qs_ * 128:(qs_ + 1) * 128]
            if kt < 4:
                if qs_ > kt:
                    blk[:] = NEG
                elif qs_ == kt:
                    blk[:] = tri0
            else:
                ktp = kt - 4
                if qs_ < ktp:
                    blk[:] = NEG
                elif qs_ == ktp:
                    blk[:] = tri1
    g["l1f_maskw"] = mw.reshape(8 * 128, 512).reshape(8, 128, 512).transpose(1, 0, 2).reshape(128, 8 * 512).astype(BF16NP)

    n = np.arange(S, dtype=np.float64)
    half = np.arange(0, LDH, 2, dtype=np.float64)
    inv_freq = 1.0 / (10000.0 ** (half / LDH))
    freqs = np.concatenate([n[:, None] * inv_freq[None, :]] * 2, axis=1)
    base = (half + 0.4 * LDH) / (1.4 * LDH)
    power = (n - S // 2) / float(WIN // 2)
    scale = base[None, :] ** power[:, None]
    scale = np.concatenate([scale, scale], axis=1)
    rot = np.stack([np.cos(freqs) * scale, np.sin(freqs) * scale,
                    np.cos(freqs) / scale, np.sin(freqs) / scale])

    x = _f32(inputs["L0_attn_x"])
    cores = []
    for cid in range(NCORES):
        b_, hf = divmod(cid, 2)
        t0 = hf * T
        per = {}
        per["x_in"] = _f32(x[b_, t0:t0 + T, :].T)
        ro = rot[:, t0:t0 + T, :]
        ro2 = np.concatenate([ro, ro], axis=2)
        per["rot_own"] = _f32(ro2.transpose(0, 2, 1).reshape(4 * 128, T))
        bpos = (np.arange(t0 - 513, t0)) % S
        rb_ = rot[:, bpos, :]
        rb2 = np.concatenate([rb_, rb_], axis=2)
        per["rot_bnd"] = _f32(rb2.transpose(0, 2, 1).reshape(4 * 128, 513))
        mcols = np.zeros((128, 4), np.float32)
        mcols[:, 0] = 1.0 if hf == 1 else 0.0
        mcols[:, 1] = 1.0 if hf == 0 else 0.0
        mcols[:, 2] = 1.0 if hf == 1 else 0.0
        per["maskcols"] = mcols

        par = hf
        patw = np.zeros((H, 3, 128, 512), np.float32)
        for h in range(H):
            for jjs in range(3):
                qd = 2 * jjs - 2 + par
                qp = 2 * jjs - 1 + par
                if 0 <= qd <= 3:
                    patw[h, jjs, :, qd * 128:(qd + 1) * 128] += g["_pat_diag"][h]
                    patw[h, jjs, :, 0:qd * 128] = NEG
                if 0 <= qp <= 3:
                    patw[h, jjs, :, qp * 128:(qp + 1) * 128] += g["_pat_prev"][h]
        per["l0_patw"] = patw.transpose(2, 0, 1, 3).reshape(128, H * 3 * 512).astype(BF16NP)

        CC1R = 6144 + 65
        KROW, VROW = 2048, 4096
        kidx = np.zeros((128, 72), np.int32)
        vidx = np.zeros((128, 72), np.int32)
        p128 = np.arange(128)
        u = 0
        for qb in range(NQB):
            for j in range(2 * qb + 2):
                t = par + 2 * j
                rk, tl = t // 16, t % 16
                kidx[:, u] = rk * CC1R + KROW + tl * 128 + p128
                vidx[:, u] = rk * CC1R + VROW + tl * 128 + p128
                u += 1
        per["l0_kidx"] = kidx
        per["l0_vidx"] = vidx
        CC2R = H * NQB * 65
        cidx = np.zeros((128, 2 * H * NTB), np.int32)
        for h in range(H):
            for i in range(NTB):
                qbg = hf * NTB + i
                base = (h * NQB + qbg) * 65
                rowp = np.minimum(p128, 64)
                cidx[:, (h * NTB + i) * 2] = hf * CC2R + base + rowp
                cidx[:, (h * NTB + i) * 2 + 1] = (1 - hf) * CC2R + base + rowp
        per["l0_cidx"] = cidx
        cores.append(per)
    return g, cores, bidx, NB



class Pool:

    def __init__(self, bld, aps, guard_sem="dC"):
        self.b = bld
        self.aps = aps
        self.guard = [0] * len(aps)
        self.i = 0
        self.sem = guard_sem

    def next(self, writer_eng):
        i = self.i % len(self.aps)
        self.i += 1
        if self.guard[i]:
            writer_eng.wait_ge(self.b.sem(self.sem), self.guard[i])
        return i, self.aps[i]

    def mark(self, i):
        self.guard[i] = self.b.cnt[self.sem]


class Builder:
    def __init__(self, bidx, nb, stop_after=None, dumps=()):
        self.bidx = bidx
        self.NB = nb
        self.stop_after = stop_after
        self.dumps = dict(dumps)
        self.dump_srcs = {}
        self.nc = bass.Bass("TRN2", target_bir_lowering=False, debug=False)
        self.cnt = {}
        self.sems = {}
        self.bank_guard = {}
        self.out_written = False

    def sem(self, name):
        if name not in self.sems:
            self.sems[name] = self.nc.alloc_semaphore(name=name)
            self.cnt[name] = 0
        return self.sems[name]

    def inc(self, instr, name, k=1):
        instr.then_inc(self.sem(name), k)
        self.cnt[name] += k
        return instr

    def wait(self, eng, name, val=None):
        self.sem(name)
        eng.wait_ge(self.sems[name], self.cnt[name] if val is None else val)

    def vs(self, ins):
        self.inc(ins, "v")
        self.nc.vector.wait_ge(self.sems["v"], self.cnt["v"])
        return ins

    def dma(self, q, out, in_, name="dX"):
        ins = q.dma_start(out=out, in_=in_)
        self.inc(ins, name, 16)
        return ins

    def t_use_bank(self, *bis):
        for bi in bis:
            g = self.bank_guard.get(bi)
            if g:
                self.nc.tensor.wait_ge(self.sems[g[0]], g[1])
                self.bank_guard[bi] = None

    def consumed_bank(self, bi, name):
        self.bank_guard[bi] = (name, self.cnt[name])

    def build(self):
        self._declare_dram()
        self._alloc_sbuf()
        self._alloc_psum()
        self._emit()
        return self.nc

    def _declare_dram(self):
        nc = self.nc
        dt_ = {}
        def din(name, shape, dtype=F32):
            dt_[name] = nc.dram_tensor(name, list(shape), dtype, kind="ExternalInput")
        din("x_in", (D, T))
        din("rot_own", (4 * 128, T))
        din("rot_bnd", (4 * 128, 513))
        din("maskcols", (128, 4))
        din("l0_patw", (128, H * 3 * 512), BF16)
        din("l0_kidx", (128, 72), I32)
        din("l0_vidx", (128, 72), I32)
        din("l0_cidx", (128, 2 * H * NTB), I32)
        for n in ("Wq", "Wk", "Wv", "Wo"):
            din(f"l0_{n}", (D, D))
            din(f"l1a_{n}", (D, D))
        din("l0_relb2", (65, 8 * 257))
        din("l0_w1T", (D, DH))
        din("l0_w2T", (DH, D))
        din("l0_convw_cols", (128, 96))
        din("biases", (128, self.NB))
        din("l0_bv_row", (1, D))
        din("l1a_bv_row", (1, D))
        for l in (0, 1):
            din(f"l1f_wqkv{l}", (D, 3 * INNER))
            din(f"l1f_wo{l}", (INNER, D))
            din(f"l1f_w1a{l}", (D, HIDP))
            din(f"l1f_w1g{l}", (D, HIDP))
            din(f"l1f_w2{l}", (HIDP, D))
        din("ident_bf", (128, 128), BF16)
        din("ident_f", (128, 128))
        din("onesrow_bf", (1, 128), BF16)
        din("onesrow_f", (1, 128))
        din("onesall_f", (128, 128))
        din("onescol_bf", (128, 1), BF16)
        din("ind2_bf", (128, 65), BF16)
        din("ones2_f", (2, 128))
        din("mask216_f", (2, 16))
        din("perm_bf", (128, 128), BF16)
        din("loc_masks", (128, 384))
        din("l1f_maskw", (128, 8 * 512), BF16)
        self.out = nc.dram_tensor("out", [D, T], F32, kind="ExternalOutput")
        self.CC1R = 6144 + 65
        self.cc1_in = nc.dram_tensor("cc1_in", [self.CC1R, 1040], BF16)
        self.cc1_out = nc.dram_tensor("cc1_out", [2 * self.CC1R, 1040], BF16)
        self.CC2R = H * NQB * 65
        self.cc2_in = nc.dram_tensor("cc2_in", [self.CC2R, TB], BF16)
        self.cc2_out = nc.dram_tensor("cc2_out", [2 * self.CC2R, TB], BF16)
        self.cc2b_in = nc.dram_tensor("cc2b_in", [128, DCH * 2 + 2], BF16)
        self.cc2b_out = nc.dram_tensor("cc2b_out", [256, DCH * 2 + 2], BF16)
        self.CC3R = 4096 + 65
        self.cc3_in = nc.dram_tensor("cc3_in", [self.CC3R, 1040], BF16)
        self.cc3_out = nc.dram_tensor("cc3_out", [2 * self.CC3R, 1040], BF16)
        self.q_hbm = nc.dram_tensor("q_hbm", [2048, 1024], BF16)
        self.o_hbm = nc.dram_tensor("o_hbm", [1024, T], BF16)
        self.cc4_in = nc.dram_tensor("cc4_in", [128, DCH * 513], BF16)
        self.cc4_out = nc.dram_tensor("cc4_out", [256, DCH * 513], BF16)
        self.cc5_in = nc.dram_tensor("cc5_in", [128, DCH * 513], BF16)
        self.cc5_out = nc.dram_tensor("cc5_out", [256, DCH * 513], BF16)
        self.x_spill = nc.dram_tensor("x_spill", [128, DCH * T], F32)
        self.vt_hbm = nc.dram_tensor("vt_hbm", [20 * 128, 520], BF16)
        self.w1q_hbm = nc.dram_tensor("w1q_hbm", [128, (D // 128) * DH], BF16)
        self.w2q_hbm = nc.dram_tensor("w2q_hbm", [128, (DH // 128) * D], BF16)
        self.dbg = {}
        for name, shape in self.dumps.items():
            self.dbg[name] = nc.dram_tensor(f"dbg_{name}", list(shape), F32, kind="ExternalOutput")
        self.din = dt_

    def _alloc_sbuf(self):
        nc = self.nc
        self.arena = nc.alloc_sbuf_tensor("arena", [128, 199168], mybir.dt.uint8)
        abase = nc.lookup_mloc(self.arena).addr

        def view(name, off, n, dtype):
            return nc.alloc_sbuf_tensor_at(name, [128, n], dtype, offset=abase + off)

        self.xs = view("xs", 0, DCH * T, F32)
        self.ab = view("ab", 65536, DCH * T, BF16)
        self.bb = view("bb", 98304, DCH * T, BF16)
        self.cb = view("cb", 131072, 12288, BF16)
        self.fs = view("fs", 155648, 4096, F32)
        self.wf = view("wf", 172032, 4096, F32)
        self.wg = view("wg", 188416, 4096, BF16)
        self.ii = view("ii", 196608, 512, I32)
        self.k2 = view("k2", 198656, 256, BF16)
        self.hg = view("hg", 65536, 45056, BF16)
        self.fsb = view("fsb", 155648, 8192, BF16)
        a = nc.alloc_sbuf_tensor
        self.ident = a("identt", [128, 128], BF16)
        self.identf = a("identft", [128, 128], F32)
        self.onesrow = a("onesrowt", [1, 128], BF16)
        self.onesrowf = a("onesrowft", [1, 128], F32)
        self.onesall = a("onesallt", [128, 128], F32)
        self.onescol = a("onescolt", [128, 1], BF16)
        self.ind2 = a("ind2t", [128, 65], BF16)
        self.ones2 = a("ones2t", [2, 128], F32)
        self.mask216 = a("mask216t", [2, 16], F32)
        self.perm = a("permt", [128, 128], BF16)
        self.locm = a("locmt", [128, 384], F32)
        self.biases = a("biasest", [128, self.NB], F32)
        self.maskc = a("maskct", [128, 4], F32)
        self.kidx = a("kidxt", [128, 72], I32)
        self.vidx = a("vidxt", [128, 72], I32)
        self.cidx = a("cidxt", [128, 2 * H * NTB], I32)
        self.rows = a("rowst", [128, 64], F32)
        self.sc2 = a("sc2t", [128, 16], F32)
        self.convw_sb = a("convwt", [128, 96], F32)

    def _alloc_psum(self):
        self.pb = [self.nc.alloc_psum_tensor(f"pb{i}", [128, 512], F32) for i in range(8)]

    def bcol(self, name, c=0):
        i = self.bidx[name] + c
        return self.biases[:, i:i + 1]

    def _emit(self):
        self.p0_prologue()
        if self.stop_after == "p0":
            return self._finish()
        self.p1_l0_attn()
        if self.stop_after in ("p1", "p1a", "p1b"):
            return self._finish()
        self.p2_l0_ffn()
        if self.stop_after == "p2":
            return self._finish()
        self.p3_l1_attn()
        if self.stop_after == "p3":
            return self._finish()
        self.p4_l1_ffn()
        self._finish()

    def p0_prologue(self):
        nc, din, sy = self.nc, self.din, self.nc.sync
        for c in range(DCH):
            self.dma(sy, self.xs[:, c * T:(c + 1) * T], din["x_in"].ap()[c * 128:(c + 1) * 128, :], "dX")
        for nm, dst in (("ident_bf", self.ident), ("ident_f", self.identf),
                        ("onesrow_bf", self.onesrow),
                        ("onesrow_f", self.onesrowf), ("onesall_f", self.onesall),
                        ("onescol_bf", self.onescol),
                        ("ind2_bf", self.ind2),
                        ("ones2_f", self.ones2), ("mask216_f", self.mask216),
                        ("perm_bf", self.perm), ("loc_masks", self.locm),
                        ("biases", self.biases), ("maskcols", self.maskc),
                        ("l0_kidx", self.kidx), ("l0_vidx", self.vidx),
                        ("l0_cidx", self.cidx), ("l0_convw_cols", self.convw_sb)):
            self.dma(sy, dst[:, :], din[nm].ap(), "dX")

    def ln_pass(self, src_bf, dst_of, tlen=T, nch=DCH, residual=None, gamma_beta=None):
        nc = self.nc
        pS, pQ, pM, pI = self.pb[4], self.pb[5], self.pb[6], self.pb[7]
        nsp = tlen // 512
        den = 1.0 / (128 * nch)
        for sp in range(nsp):
            sl = slice(sp * 512, sp * 512 + 512)
            self.t_use_bank(4, 5)
            for c in range(nch):
                xb = src_bf(c)[:, sl]
                sq = self.wg[:, (c % 2) * 512:(c % 2) * 512 + 512]
                self.wait(nc.vector, "t")
                ins = nc.vector.tensor_tensor(sq, xb, xb, ALU.mult)
                self.inc(ins, "v")
                nc.tensor.wait_ge(self.sems["v"], self.cnt["v"])
                nc.tensor.matmul(pS[0:1, 0:512], self.onescol[:, :], xb,
                                 start=(c == 0), stop=(c == nch - 1))
                ins = nc.tensor.matmul(pQ[0:1, 0:512], self.onescol[:, :], sq,
                                       start=(c == 0), stop=(c == nch - 1))
                self.inc(ins, "t")
            self.wait(nc.vector, "t")
            r_mu = self.fs[0:1, 0:512]
            r_inv = self.fs[0:1, 512:1024]
            r_mi = self.fs[0:1, 1024:1536]
            self.vs(nc.vector.tensor_scalar(r_mu, pS[0:1, 0:512], den, None, ALU.mult))
            self.vs(nc.vector.tensor_tensor(r_inv, r_mu, r_mu, ALU.mult))
            self.vs(nc.vector.tensor_scalar(r_inv, r_inv, float(EPS), None, ALU.subtract))
            self.vs(nc.vector.scalar_tensor_tensor(r_inv, pQ[0:1, 0:512], den, r_inv,
                                                   ALU.mult, ALU.subtract))
            self.consumed_bank(4, "v"); self.consumed_bank(5, "v")
            self.wait(nc.scalar, "v")
            ins = nc.scalar.activation(r_inv, r_inv, AF.Sqrt)
            self.inc(ins, "s")
            self.wait(nc.vector, "s")
            self.vs(nc.vector.reciprocal(r_inv, r_inv))
            self.vs(nc.vector.tensor_tensor(r_mi, r_mu, r_inv, ALU.mult))
            self.t_use_bank(6, 7)
            nc.tensor.wait_ge(self.sems["v"], self.cnt["v"])
            nc.tensor.matmul(pI[:, 0:512], self.onesrowf[:, :], r_inv, start=True, stop=True)
            ins = nc.tensor.matmul(pM[:, 0:512], self.onesrowf[:, :], r_mi, start=True, stop=True)
            self.inc(ins, "t")
            self.wait(nc.vector, "t")
            for c in range(nch):
                xb = src_bf(c)[:, sl]
                tmp = self.wf[:, (c % 2) * 512:(c % 2) * 512 + 512]
                nc.vector.tensor_tensor(tmp, xb, pI[:, 0:512], ALU.mult)
                dst = dst_of(c, sp)
                if residual is None and gamma_beta is None:
                    ins = nc.vector.tensor_tensor(dst, tmp, pM[:, 0:512], ALU.subtract)
                elif gamma_beta is not None:
                    nc.vector.tensor_tensor(tmp, tmp, pM[:, 0:512], ALU.subtract)
                    gcol, bcolv = gamma_beta(c)
                    ins = nc.vector.tensor_scalar(dst, tmp, gcol, bcolv, ALU.mult, ALU.add)
                else:
                    nc.vector.tensor_tensor(tmp, tmp, pM[:, 0:512], ALU.subtract)
                    ins = nc.vector.tensor_add(dst, residual(c, sp), tmp)
                self.inc(ins, "v")
            self.consumed_bank(6, "v"); self.consumed_bank(7, "v")

    def stream_w(self, dram, rows, col0, ncols, wg_off=0, src_bf16=False):
        nc = self.nc
        nk = rows // 128
        n = nk * ncols
        assert n <= 4096
        if src_bf16:
            src = dram.ap()[:, :]
            raise NotImplementedError
        src = dram.ap()[:, col0:col0 + ncols].rearrange("(k p) m -> p k m", p=128)
        dst = self.wf[:, 0:n].rearrange("p (k m) -> p k m", k=nk)
        self.wait(nc.sync, "v")
        self.dma(nc.sync, dst, src, "dW")
        self.wait(nc.vector, "dW")
        self.wait(nc.vector, "t")
        ins = nc.vector.tensor_copy(self.wg[:, wg_off:wg_off + n], self.wf[:, 0:n])
        self.inc(ins, "v")
        nc.tensor.wait_ge(self.sems["v"], self.cnt["v"])
        return [self.wg[:, wg_off + k * ncols: wg_off + (k + 1) * ncols] for k in range(nk)]

    def proj(self, wdram, din_, dout_, rhs_of, evac, tlen=T, mgrp=4, banks=(0, 1, 2, 3)):
        nc = self.nc
        nk = din_ // 128
        nm = dout_ // 128
        nsp = tlen // 512
        for m0 in range(0, nm, mgrp):
            mg = min(mgrp, nm - m0)
            pieces = self.stream_w(wdram, din_, m0 * 128, mg * 128)
            for mi in range(mg):
                m = m0 + mi
                for sp in range(nsp):
                    bi = banks[(m * nsp + sp) % len(banks)]
                    self.t_use_bank(bi)
                    for k in range(nk):
                        lhsT = pieces[k][:, mi * 128:(mi + 1) * 128]
                        ins = nc.tensor.matmul(self.pb[bi][:, 0:512], lhsT,
                                               rhs_of(k)[:, sp * 512:sp * 512 + 512],
                                               start=(k == 0), stop=(k == nk - 1))
                    self.inc(ins, "t")
                    evac(m, sp, self.pb[bi][:, 0:512], bi)

    def bcast_rows_to_fs(self, row_ap, fs_off, n):
        nc = self.nc
        self.t_use_bank(6)
        nc.tensor.wait_ge(self.sems["v"], self.cnt["v"])
        ins = nc.tensor.matmul(self.pb[6][:, 0:n], self.onesrowf[:, :], row_ap, start=True, stop=True)
        self.inc(ins, "t")
        self.wait(nc.vector, "t")
        ins = nc.vector.tensor_copy(self.fs[:, fs_off:fs_off + n], self.pb[6][:, 0:n])
        self.inc(ins, "v")
        self.consumed_bank(6, "v")

    def p1_l0_attn(self):
        nc = self.nc
        din = self.din
        for e in (nc.vector, nc.tensor, nc.scalar, nc.gpsimd):
            self.wait(e, "dX")

        for c in range(DCH):
            ins = nc.vector.tensor_copy(self.bb[:, c * T:(c + 1) * T], self.xs[:, c * T:(c + 1) * T])
            self.inc(ins, "v")
        xsrc = lambda c: self.bb[:, c * T:(c + 1) * T]
        self.ln_pass(xsrc, lambda c, sp: self.ab[:, c * T + sp * 512: c * T + sp * 512 + 512])
        if "xl" in self.dumps:
            self.dump_srcs["xl"] = self.ab[:, :]

        bvf = self.fs[0:1, 2048:2048 + 512]
        for halfc in range(2):
            self.wait(nc.sync, "t")
            self.dma(nc.sync, bvf, din["l0_bv_row"].ap()[:, halfc * 512:halfc * 512 + 512], "dX")
            self.wait(nc.tensor, "dX")
            self.bcast_rows_to_fs(bvf, 2560 + halfc * 512, 512)

        bq8 = self.fs[:, 2048:2048 + 8]
        self.wait(nc.vector, "t")
        ins = nc.vector.tensor_scalar(bq8, self.biases[:, self.bidx["l0_bq"]:self.bidx["l0_bq"] + 8],
                                      0.125, None, ALU.mult)
        self.inc(ins, "v")

        KROW, VROW, TROW = 2048, 4096, 6144
        stage = Pool(self, [self.cb[:, i * 512:(i + 1) * 512] for i in range(8)])

        def rhs_xl(k):
            return self.ab[:, k * T:(k + 1) * T]

        def mk_evac_q(bias_of, scale):
            def evac(m, sp, ps, bi):
                self.wait(nc.vector, "t")
                si, st = stage.next(nc.vector)
                ins = nc.vector.tensor_scalar(st, ps, float(scale), bias_of(m), ALU.mult, ALU.add)
                self.inc(ins, "v")
                self.consumed_bank(bi, "v")
                self.wait(nc.gpsimd, "v")
                rowb = (m * 2 + sp // 2) * 128
                colb = (sp % 2) * 512
                self.dma(nc.gpsimd, self.cc1_in.ap()[rowb:rowb + 128, colb:colb + 512], st, "dC")
                stage.mark(si)
            return evac

        self.proj(din["l0_Wq"], D, D, rhs_xl, mk_evac_q(lambda m: bq8[:, m:m + 1], 0.125))

        self.kn_ctr = 0
        def evac_k(m, sp, ps, bi):
            self.wait(nc.vector, "t")
            si, st = stage.next(nc.vector)
            ins = nc.vector.tensor_scalar(st, ps, self.bcol("l0_bk", m), None, ALU.add)
            self.inc(ins, "v")
            self.consumed_bank(bi, "v")
            self.wait(nc.gpsimd, "v")
            kdst = self.cc1_in.ap()[KROW + sp * 512:KROW + (sp + 1) * 512, m * 128:(m + 1) * 128]
            kdst = kdst.rearrange("(t p) k -> p t k", p=128)
            self.dma(nc.gpsimd, kdst, st.rearrange("p (t k) -> p t k", k=128), "dC")
            stage.mark(si)
            sq = self.wg[:, 2048 + (self.kn_ctr % 2) * 512: 2048 + (self.kn_ctr % 2) * 512 + 512]
            self.wait(nc.vector, "t")
            ins = nc.vector.tensor_tensor(sq, st, st, ALU.mult)
            self.inc(ins, "v")
            self.t_use_bank(5)
            nc.tensor.wait_ge(self.sems["v"], self.cnt["v"])
            ins = nc.tensor.matmul(self.pb[5][0:65, 0:512], self.ind2[:, :], sq, start=True, stop=True)
            self.inc(ins, "t")
            self.wait(nc.vector, "t")
            ins = nc.vector.reduce_max(self.fs[0:65, 3584 + self.kn_ctr:3584 + self.kn_ctr + 1],
                                       self.pb[5][0:65, 0:512], axis=AX.X)
            self.inc(ins, "v")
            self.consumed_bank(5, "v")
            self.kn_ctr += 1
        self.proj(din["l0_Wk"], D, D, rhs_xl, evac_k, mgrp=2)
        self.wait(nc.vector, "t")
        for c in range(DCH):
            ins = nc.vector.reduce_max(self.k2[0:65, c:c + 1],
                                       self.fs[0:65, 3584 + c * NTB:3584 + (c + 1) * NTB], axis=AX.X)
            self.inc(ins, "v")
        self.wait(nc.gpsimd, "v")
        self.dma(nc.gpsimd, self.cc1_in.ap()[TROW:TROW + 65, 0:16], self.k2[0:65, 0:16], "dC")

        onecols = self.k2[:, 64:80]
        ins = nc.vector.memset(onecols, 1.0)
        self.inc(ins, "v")
        self.wait(nc.gpsimd, "v")
        with nc.allow_non_contiguous_dma("ones columns in Vt payload (tiny)"):
            for tt in range(16):
                dstv = self.cc1_in.ap()[VROW + tt * 128:VROW + (tt + 1) * 128, :]
                dstv = dstv.rearrange("p (h s) -> p h s", s=65)[:, :, 64:65]
                self.dma(nc.gpsimd, dstv, onecols.rearrange("p (h s) -> p h s", s=1), "dC")
        for halfc in range(2):
            pieces = self.stream_w(din["l0_Wv"], D, halfc * 512, 512)
            for tt in range(16):
                bi = tt % 2
                self.t_use_bank(bi)
                for k in range(DCH):
                    lhsT = self.ab[:, k * T + tt * 128: k * T + tt * 128 + 128]
                    ins = nc.tensor.matmul(self.pb[bi][:, 0:512], lhsT, pieces[k],
                                           start=(k == 0), stop=(k == DCH - 1))
                self.inc(ins, "t")
                self.wait(nc.vector, "t")
                si, st = stage.next(nc.vector)
                ins = nc.vector.tensor_tensor(st, self.pb[bi][:, 0:512],
                                              self.fs[:, 2560 + halfc * 512:2560 + halfc * 512 + 512],
                                              ALU.add)
                self.inc(ins, "v")
                self.consumed_bank(bi, "v")
                self.wait(nc.gpsimd, "v")
                dstv = self.cc1_in.ap()[VROW + tt * 128:VROW + (tt + 1) * 128, :]
                dstv = dstv.rearrange("p (h s) -> p h s", s=65)[:, halfc * 8:(halfc + 1) * 8, 0:64]
                self.dma(nc.gpsimd, dstv, st.rearrange("p (h s) -> p h s", s=64), "dC")
                stage.mark(si)

        g = nc.gpsimd
        self.wait(g, "dC")
        ins = g.collective_compute("AllGather", ALU.bypass, replica_groups=REPL,
                                   ins=[self.cc1_in.ap().opt()], outs=[self.cc1_out.ap().opt()])
        self.inc(ins, "CC")
        self.wait(g, "CC")

        if "cc1" in self.dumps:
            self.dump_srcs["cc1"] = ("dram", self.cc1_out.ap())
        if self.stop_after == "p1a":
            return
        self.p1b_flash()
        if self.stop_after == "p1b":
            return
        self.p1c_combine_wo()

    def p1b_flash(self):
        nc = self.nc
        din = self.din
        TROW = 6144
        self.wait(nc.sync, "t")
        self.wait(nc.sync, "v")
        self.dma(nc.sync, self.fs[0:65, 0:2056], din["l0_relb2"].ap(), "dX")
        self.wait(nc.vector, "dX")
        rfar = self.rows[0:65, 8:16]
        rmax = self.rows[0:65, 16:24]
        for c in range(8):
            ins = nc.vector.tensor_copy(rfar[:, c:c + 1], self.fs[0:65, c * 257 + 256:c * 257 + 257])
            self.inc(ins, "v")
        for c in range(8):
            ins = nc.vector.reduce_max(rmax[:, c:c + 1], self.fs[0:65, c * 257:(c + 1) * 257], axis=AX.X)
            self.inc(ins, "v")
        self.wait(nc.gpsimd, "CC")
        self.dma(nc.gpsimd, self.k2[0:65, 16:32], self.cc1_out.ap()[TROW:TROW + 65, 0:16], "dC")
        self.dma(nc.gpsimd, self.k2[0:65, 32:48],
                 self.cc1_out.ap()[self.CC1R + TROW:self.CC1R + TROW + 65, 0:16], "dC")
        self.wait(nc.vector, "dC")
        self.vs(nc.vector.tensor_tensor(self.rows[0:65, 0:8], self.k2[0:65, 16:24],
                                        self.k2[0:65, 32:40], ALU.max))
        self.wait(nc.scalar, "v")
        ins = nc.scalar.activation(self.rows[0:65, 0:8], self.rows[0:65, 0:8], AF.Sqrt)
        self.inc(ins, "s")
        pO = (self.pb[0], self.pb[1])
        pS_ = (self.pb[2], self.pb[3])
        qstage = self.bb[:, 0:4096]
        ptslots = [self.bb[:, 4608 + i * 512: 4608 + (i + 1) * 512] for i in range(4)]
        pt_i = 0

        ug = 0
        import os as _os
        _qbmax = int(_os.environ.get("MK_QBMAX", NQB))
        for qb in range(_qbmax):
            rank = qb // 4
            cnt = 2 * qb + 2
            u2, u1 = (qb % 4) // 2, (qb % 4) % 2
            self.wait(nc.sync, "v")
            self.wait(nc.sync, "t")
            for c in range(DCH):
                rowb = rank * self.CC1R + (c * 2 + u2) * 128
                self.dma(nc.sync, qstage[:, c * 512:(c + 1) * 512],
                         self.cc1_out.ap()[rowb:rowb + 128, u1 * 512:(u1 + 1) * 512], "dK")
            self.wait(nc.vector, "dK")
            for hp in range(DCH):
                sq = self.bb[:, 4096:4608]
                self.wait(nc.vector, "t")
                ins = nc.vector.tensor_tensor(sq, qstage[:, hp * 512:(hp + 1) * 512],
                                              qstage[:, hp * 512:(hp + 1) * 512], ALU.mult)
                self.inc(ins, "v")
                self.t_use_bank(5)
                nc.tensor.wait_ge(self.sems["v"], self.cnt["v"])
                ins = nc.tensor.matmul(self.pb[5][0:65, 0:512], self.ind2[:, :], sq, start=True, stop=True)
                self.inc(ins, "t")
                self.wait(nc.vector, "t")
                ins = nc.vector.reduce_max(self.rows[0:65, 24 + hp:25 + hp], self.pb[5][0:65, 0:512],
                                           axis=AX.X)
                self.inc(ins, "v")
                self.consumed_bank(5, "v")
            self.wait(nc.scalar, "v")
            ins = nc.scalar.activation(self.rows[0:65, 24:32], self.rows[0:65, 24:32], AF.Sqrt)
            self.inc(ins, "s")
            self.wait(nc.vector, "s")
            bv = self.rows[0:65, 32:40]
            self.vs(nc.vector.tensor_tensor(bv, self.rows[0:65, 24:32], self.rows[0:65, 0:8], ALU.mult))
            self.vs(nc.vector.tensor_tensor(bv, bv, self.rows[0:65, 16:24], ALU.add))
            self.vs(nc.vector.tensor_tensor(bv, self.rows[0:65, 8:16], bv, ALU.subtract))
            self.t_use_bank(7)
            nc.tensor.wait_ge(self.sems["v"], self.cnt["v"])
            for j in range(16):
                rr, cc_ = (j % 2) * 64, j // 2
                ins = nc.tensor.matmul(self.pb[7][:, j:j + 1], self.onesall[rr:rr + 1, 0:128],
                                       bv[rr:rr + 1, cc_:cc_ + 1], start=True, stop=True)
            self.inc(ins, "t")
            self.wait(nc.vector, "t")
            self.wait(nc.vector, "s")
            ins = nc.vector.tensor_copy(self.fs[:, 3616:3632], self.pb[7][:, 0:16])
            self.inc(ins, "v")
            self.consumed_bank(7, "v")
            biascols = self.fs[:, 3616:3632]

            nhalf = (cnt + 7) // 8
            for half in range(nhalf):
                j0 = half * 8
                ch = min(cnt - j0, 8)
                self.wait(nc.gpsimd, "t")
                for jj in range(ch):
                    u = ug + j0 + jj
                    ins = nc.gpsimd.indirect_dma_start(
                        self.ab[:, jj * 1040:jj * 1040 + 1040], None,
                        self.cc1_out.ap(),
                        IndirectOffsetOnAxis(self.kidx[:, u:u + 1], 0))
                    self.inc(ins, "dK", 16)
                    ins = nc.gpsimd.indirect_dma_start(
                        self.cb[:, jj * 1040:(jj + 1) * 1040], None,
                        self.cc1_out.ap(),
                        IndirectOffsetOnAxis(self.vidx[:, u:u + 1], 0))
                    self.inc(ins, "dK", 16)
                self.wait(nc.tensor, "dK")
                for hp in range(DCH):
                    hA, hB = 2 * hp, 2 * hp + 1
                    self.wait(nc.sync, "v")
                    self.dma(nc.sync, self.ab[:, 8320 + (hp % 2) * 3072: 8320 + (hp % 2) * 3072 + 3072],
                             din["l0_patw"].ap()[:, hA * 1536:hA * 1536 + 3072], "dP")
                    self.wait(nc.vector, "dP")
                    for jj in range(ch):
                        j = j0 + jj
                        kbase = jj * 1040 + hp * 128
                        lA = self.ab[0:64, kbase:kbase + 128]
                        lB = self.ab[64:128, kbase:kbase + 128]
                        self.t_use_bank(2, 3)
                        nc.tensor.matmul(pS_[0][:, 0:512], lA, qstage[0:64, hp * 512:(hp + 1) * 512],
                                         start=True, stop=True)
                        ins = nc.tensor.matmul(pS_[1][:, 0:512], lB, qstage[64:128, hp * 512:(hp + 1) * 512],
                                               start=True, stop=True)
                        self.inc(ins, "t")
                        if 2 * qb - 1 <= j <= 2 * qb + 1:
                            jjs = j - (2 * qb - 1)
                            self.wait(nc.vector, "t")
                            pb_ = 8320 + (hp % 2) * 3072
                            for bi2, hrel in ((0, 0), (1, 1)):
                                pa = self.ab[:, pb_ + hrel * 1536 + jjs * 512: pb_ + hrel * 1536 + jjs * 512 + 512]
                                ins = nc.vector.tensor_tensor(pS_[bi2][:, 0:512], pS_[bi2][:, 0:512],
                                                              pa, ALU.add)
                            self.inc(ins, "v")
                            nc.scalar.wait_ge(self.sems["v"], self.cnt["v"])
                        self.wait(nc.scalar, "t")
                        ptA = ptslots[pt_i % 4]; ptB = ptslots[(pt_i + 1) % 4]; pt_i += 2
                        ins = nc.scalar.activation(ptA, pS_[0][:, 0:512], AF.Exp,
                                                   bias=biascols[:, hA:hA + 1], scale=1.0)
                        ins = nc.scalar.activation(ptB, pS_[1][:, 0:512], AF.Exp,
                                                   bias=biascols[:, hB:hB + 1], scale=1.0)
                        self.inc(ins, "s")
                        self.consumed_bank(2, "s"); self.consumed_bank(3, "s")
                        if jj == 0:
                            self.t_use_bank(0, 1)
                        self.wait(nc.tensor, "s")
                        vA = self.cb[:, jj * 1040 + hA * 65: jj * 1040 + hA * 65 + 65]
                        vB = self.cb[:, jj * 1040 + hB * 65: jj * 1040 + hB * 65 + 65]
                        nc.tensor.matmul(pO[0][0:65, 0:512], vA, ptA, start=(jj == 0), stop=(jj == ch - 1))
                        ins = nc.tensor.matmul(pO[1][0:65, 0:512], vB, ptB, start=(jj == 0), stop=(jj == ch - 1))
                        self.inc(ins, "t")
                    self.wait(nc.vector, "t")
                    accA = self.bb[0:65, 8192 + hA * 512: 8192 + hA * 512 + 512]
                    accB = self.bb[0:65, 8192 + hB * 512: 8192 + hB * 512 + 512]
                    if half == 0 and nhalf > 1:
                        nc.vector.tensor_copy(accA, pO[0][0:65, 0:512])
                        ins = nc.vector.tensor_copy(accB, pO[1][0:65, 0:512])
                    elif nhalf > 1:
                        nc.vector.tensor_add(accA, pO[0][0:65, 0:512], accA)
                        ins = nc.vector.tensor_add(accB, pO[1][0:65, 0:512], accB)
                    else:
                        nc.vector.tensor_copy(accA, pO[0][0:65, 0:512])
                        ins = nc.vector.tensor_copy(accB, pO[1][0:65, 0:512])
                    self.inc(ins, "v")
                    self.consumed_bank(0, "v"); self.consumed_bank(1, "v")
                    if half == nhalf - 1:
                        self.wait(nc.gpsimd, "v")
                        for hh, acc in ((hA, accA), (hB, accB)):
                            rb = (hh * NQB + qb) * 65
                            self.dma(nc.gpsimd, self.cc2_in.ap()[rb:rb + 65, 0:512], acc, "dC")
            ug += cnt
        g = nc.gpsimd
        self.wait(g, "dC")
        ins = g.collective_compute("AllGather", ALU.bypass, replica_groups=REPL,
                                   ins=[self.cc2_in.ap().opt()], outs=[self.cc2_out.ap().opt()])
        self.inc(ins, "CC")
        self.wait(g, "CC")
        if "cc2" in self.dumps:
            self.dump_srcs["cc2"] = ("dram", self.cc2_out.ap())

    def p1c_combine_wo(self):
        nc = self.nc
        din = self.din
        for hh in range(H):
            for i in range(NTB):
                own = self.bb[0:128, 0:512]
                par = self.bb[0:128, 512:1024]
                self.wait(nc.gpsimd, "v")
                ci = (hh * NTB + i) * 2
                ins = nc.gpsimd.indirect_dma_start(
                    own, None, self.cc2_out.ap()[:, 0:512],
                    IndirectOffsetOnAxis(self.cidx[:, ci:ci + 1], 0))
                self.inc(ins, "dC", 16)
                ins = nc.gpsimd.indirect_dma_start(
                    par, None, self.cc2_out.ap()[:, 0:512],
                    IndirectOffsetOnAxis(self.cidx[:, ci + 1:ci + 2], 0))
                self.inc(ins, "dC", 16)
                self.wait(nc.vector, "dC")
                comb = self.fs[0:65, 0:512]
                ins = nc.vector.tensor_add(comb, own[0:65, :], par[0:65, :])
                self.inc(ins, "v")
                rec = self.fs[64:65, 0:512]
                ins = nc.vector.reciprocal(rec, rec)
                self.inc(ins, "v")
                self.t_use_bank(6)
                nc.tensor.wait_ge(self.sems["v"], self.cnt["v"])
                ins = nc.tensor.matmul(self.pb[6][0:64, 0:512], self.onesall[64:65, 0:64], rec,
                                       start=True, stop=True)
                self.inc(ins, "t")
                self.wait(nc.vector, "t")
                c, rr = hh // 2, (hh % 2) * 64
                dst = self.ab[rr:rr + 64, c * T + i * 512: c * T + i * 512 + 512]
                ins = nc.vector.tensor_tensor(dst, self.fs[0:64, 0:512], self.pb[6][0:64, 0:512], ALU.mult)
                self.inc(ins, "v")
                self.consumed_bank(6, "v")
        if "o" in self.dumps:
            self.dump_srcs["o"] = self.ab[:, :]

        stage = Pool(self, [self.cb[:, i * 512:(i + 1) * 512] for i in range(4)], "v")

        def rhs_o(k):
            return self.ab[:, k * T:(k + 1) * T]

        def evac_wo(m, sp, ps, bi):
            self.wait(nc.vector, "t")
            dst = self.bb[:, m * T + sp * 512: m * T + sp * 512 + 512]
            ins = nc.vector.tensor_scalar(dst, ps, self.bcol("l0_bo", m), None, ALU.add)
            self.inc(ins, "v")
            self.consumed_bank(bi, "v")

        self.proj(din["l0_Wo"], D, D, rhs_o, evac_wo)
        if "attnout" in self.dumps:
            self.dump_srcs["attnout"] = self.bb[:, :]
        src_bf = lambda c: self.bb[:, c * T:(c + 1) * T]
        self.ln_pass(src_bf, lambda c, sp: self.xs[:, c * T + sp * 512: c * T + sp * 512 + 512],
                     residual=lambda c, sp: self.xs[:, c * T + sp * 512: c * T + sp * 512 + 512])
        if "x1" in self.dumps:
            self.dump_srcs["x1"] = self.xs[:, :]

    def wquant(self, wdram, rows, cols, qdram, scidx):
        nc = self.nc
        nk = rows // 128
        ncs = 4096 // nk
        nslab = cols // ncs
        acc = self.sc2[:, 8:9]
        for j in range(nslab):
            src = wdram.ap()[:, j * ncs:(j + 1) * ncs].rearrange("(k p) m -> p k m", p=128)
            self.wait(nc.sync, "v")
            self.dma(nc.sync, self.wf[:, 0:4096].rearrange("p (k m) -> p k m", k=nk), src, "dW")
            self.wait(nc.vector, "dW")
            self.vs(nc.vector.tensor_reduce(self.sc2[:, 9:10], self.wf[:, 0:4096], axis=AX.X,
                                            op=ALU.add, apply_absolute_value=True))
            if j == 0:
                self.vs(nc.vector.tensor_copy(acc, self.sc2[:, 9:10]))
            else:
                self.vs(nc.vector.tensor_add(acc, acc, self.sc2[:, 9:10]))
        self.t_use_bank(5)
        nc.tensor.wait_ge(self.sems["v"], self.cnt["v"])
        ins = nc.tensor.matmul(self.pb[5][0:1, 0:1], acc, self.onesall[:, 0:1], start=True, stop=True)
        self.inc(ins, "t")
        self.wait(nc.vector, "t")
        mrow = self.sc2[0:1, 10:11]
        self.vs(nc.vector.tensor_scalar(mrow, self.pb[5][0:1, 0:1], 1.0 / (rows * cols), None, ALU.mult))
        srrow = self.sc2[0:1, 11:13]
        self.vs(nc.vector.tensor_scalar(srrow[:, 0:1], mrow, 1e-5, None, ALU.max))
        self.vs(nc.vector.tensor_copy(srrow[:, 1:2], srrow[:, 0:1]))
        self.vs(nc.vector.reciprocal(srrow, srrow))
        srrow = srrow[:, 0:1]
        self.consumed_bank(5, "v")
        self.t_use_bank(5)
        nc.tensor.wait_ge(self.sems["v"], self.cnt["v"])
        nc.tensor.matmul(self.pb[5][:, 0:1], self.onesrowf[:, :], mrow, start=True, stop=True)
        ins = nc.tensor.matmul(self.pb[5][:, 1:2], self.onesrowf[:, :], srrow, start=True, stop=True)
        self.inc(ins, "t")
        self.wait(nc.vector, "t")
        nc.vector.tensor_copy(self.sc2[:, scidx * 2:scidx * 2 + 1], self.pb[5][:, 0:1])
        ins = nc.vector.tensor_copy(self.sc2[:, scidx * 2 + 1:scidx * 2 + 2], self.pb[5][:, 1:2])
        self.inc(ins, "v")
        self.consumed_bank(5, "v")
        srcol = self.sc2[:, scidx * 2 + 1:scidx * 2 + 2]
        for j in range(nslab):
            src = wdram.ap()[:, j * ncs:(j + 1) * ncs].rearrange("(k p) m -> p k m", p=128)
            self.wait(nc.sync, "v")
            self.dma(nc.sync, self.wf[:, 0:4096].rearrange("p (k m) -> p k m", k=nk), src, "dW")
            self.wait(nc.vector, "dW")
            self.wait(nc.vector, "dC")
            for pc in range(8):
                sl = slice(pc * 512, pc * 512 + 512)
                tmp = self.fs[:, 0:512]
                nc.vector.tensor_scalar(tmp, self.wf[:, sl], srcol, None, ALU.mult)
                nc.vector.tensor_copy(self.ii[:, 0:512], tmp)
                nc.vector.tensor_copy(self.wg[:, sl], self.ii[:, 0:512])
                ins = nc.vector.tensor_scalar(self.wg[:, sl], self.wg[:, sl], 1.0, -1.0,
                                              ALU.min, ALU.max)
                self.inc(ins, "v")
            self.wait(nc.gpsimd, "v")
            for k in range(nk):
                self.dma(nc.gpsimd, qdram.ap()[:, k * cols + j * ncs: k * cols + (j + 1) * ncs],
                         self.wg[:, k * ncs:(k + 1) * ncs], "dC")

    def p2_l0_ffn(self):
        nc, din = self.nc, self.din
        KQ = self.w1q_hbm
        self.wquant(din["l0_w1T"], D, DH, self.w1q_hbm, 0)
        self.wquant(din["l0_w2T"], DH, D, self.w2q_hbm, 1)

        for c in range(DCH):
            self.wait(nc.vector, "dC")
            ins = nc.vector.tensor_copy(self.bb[:, c * T:(c + 1) * T], self.xs[:, c * T:(c + 1) * T])
            self.inc(ins, "v")
        self.ln_pass(lambda c: self.bb[:, c * T:(c + 1) * T],
                     lambda c, sp: self.ab[:, c * T + sp * 512: c * T + sp * 512 + 512])
        if "ln2" in self.dumps:
            self.dump_srcs["ln2"] = self.ab[:, :]

        for blk in range(NTB):
            bsl = slice(blk * 512, blk * 512 + 512)
            acc = self.fs[:, 0:512]
            self.wait(nc.vector, "t")
            for c in range(DCH):
                lnc = self.ab[:, c * T + blk * 512: c * T + blk * 512 + 512]
                absb = self.bb[:, blk * 512:blk * 512 + 512]
                self.wait(nc.scalar, "v")
                ins = nc.scalar.activation(absb, lnc, AF.Abs)
                self.inc(ins, "s")
                self.wait(nc.vector, "s")
                if c == 0:
                    ins = nc.vector.tensor_copy(acc, absb)
                else:
                    ins = nc.vector.tensor_max(acc, acc, absb)
                self.inc(ins, "v")
            srow = self.fs[0:1, 512:1024]
            for i4 in range(4):
                self.t_use_bank(5)
                nc.tensor.wait_ge(self.sems["v"], self.cnt["v"])
                ins = nc.tensor.transpose(self.pb[5][:, 0:128], acc[:, i4 * 128:(i4 + 1) * 128],
                                          self.identf[:, :])
                self.inc(ins, "t")
                self.wait(nc.vector, "t")
                ins = nc.vector.reduce_max(self.fs[:, 1024 + i4:1025 + i4], self.pb[5][:, 0:128], axis=AX.X)
                self.inc(ins, "v")
                self.consumed_bank(5, "v")
            self.t_use_bank(5)
            nc.tensor.wait_ge(self.sems["v"], self.cnt["v"])
            for i4 in range(4):
                ins = nc.tensor.transpose(self.pb[5][0:1, i4 * 128:(i4 + 1) * 128],
                                          self.fs[:, 1024 + i4:1025 + i4], self.identf[:, :])
            self.inc(ins, "t")
            self.wait(nc.vector, "t")
            self.vs(nc.vector.tensor_scalar(srow, self.pb[5][0:1, 0:512], 1e-5, None, ALU.max))
            self.consumed_bank(5, "v")
            sclrow = self.fs[0:1, 1024:1536]
            self.vs(nc.vector.reciprocal(sclrow, srow))
            self.vs(nc.vector.tensor_scalar(sclrow, sclrow, 127.0, None, ALU.mult))
            einvrow = self.fs[0:1, 1536:2048]
            self.vs(nc.vector.tensor_scalar(einvrow, srow, 1.0 / 127.0, None, ALU.mult))
            if blk == 0:
                self.vs(nc.vector.tensor_copy(self.k2[0:1, 144:145], einvrow[:, 0:1]))
            if blk == NTB - 1:
                self.vs(nc.vector.tensor_copy(self.k2[0:1, 145:146], einvrow[:, 511:512]))
            self.t_use_bank(6, 7)
            nc.tensor.wait_ge(self.sems["v"], self.cnt["v"])
            nc.tensor.matmul(self.pb[6][:, 0:512], self.onesrowf[:, :], sclrow, start=True, stop=True)
            ins = nc.tensor.matmul(self.pb[7][:, 0:512], self.onesrowf[:, :], einvrow, start=True, stop=True)
            self.inc(ins, "t")
            self.wait(nc.vector, "t")
            ins = nc.vector.tensor_scalar(self.fs[:, 2048 + blk * 512:2560 + blk * 512],
                                          self.pb[7][:, 0:512], self.sc2[:, 0:1], None, ALU.mult)
            self.inc(ins, "v")
            self.consumed_bank(7, "v")
            for c in range(DCH):
                lnc = self.ab[:, c * T + blk * 512: c * T + blk * 512 + 512]
                tmp = self.fs[:, 0:512]
                self.wait(nc.vector, "t")
                nc.vector.tensor_tensor(tmp, lnc, self.pb[6][:, 0:512], ALU.mult)
                nc.vector.tensor_copy(self.ii[:, 0:512], tmp)
                ins = nc.vector.tensor_copy(self.bb[:, c * T + blk * 512: c * T + blk * 512 + 512],
                                            self.ii[:, 0:512])
                self.inc(ins, "v")
            self.consumed_bank(6, "v")
        if "xq" in self.dumps:
            self.dump_srcs["xq"] = self.bb[:, :]

        self.wait(nc.gpsimd, "v")
        with nc.allow_non_contiguous_dma("xq boundary columns (tiny)"):
            src_first = self.bb[:, 0:DCH * T].rearrange("p (c t) -> p c t", t=T)[:, :, 0:1]
            src_last = self.bb[:, 0:DCH * T].rearrange("p (c t) -> p c t", t=T)[:, :, T - 1:T]
            self.dma(nc.gpsimd, self.cc2b_in.ap()[:, 0:8].rearrange("p (c o) -> p c o", o=1),
                     src_first, "dC")
            self.dma(nc.gpsimd, self.cc2b_in.ap()[:, 8:16].rearrange("p (c o) -> p c o", o=1),
                     src_last, "dC")
        self.dma(nc.gpsimd, self.cc2b_in.ap()[0:1, 16:18], self.k2[0:1, 144:146], "dC")
        g = nc.gpsimd
        self.wait(g, "dC")
        ins = g.collective_compute("AllGather", ALU.bypass, replica_groups=REPL,
                                   ins=[self.cc2b_in.ap().opt()], outs=[self.cc2b_out.ap().opt()])
        self.inc(ins, "CC")
        self.wait(g, "CC")
        self.dma(g, self.k2[:, 96:104], self.cc2b_out.ap()[0:128, 8:16], "dC")
        self.dma(g, self.k2[:, 104:112], self.cc2b_out.ap()[128:256, 0:8], "dC")
        self.dma(g, self.k2[0:1, 146:147], self.cc2b_out.ap()[0:1, 17:18], "dC")
        self.dma(g, self.k2[0:1, 147:148], self.cc2b_out.ap()[128:129, 16:17], "dC")
        self.wait(nc.vector, "dC")
        for c in range(DCH):
            nc.vector.tensor_scalar(self.k2[:, 112 + c:113 + c], self.k2[:, 96 + c:97 + c],
                                    self.maskc[:, 0:1], None, ALU.mult)
            ins = nc.vector.tensor_scalar(self.k2[:, 120 + c:121 + c], self.k2[:, 104 + c:105 + c],
                                          self.maskc[:, 1:2], None, ALU.mult)
            self.inc(ins, "v")
        self.vs(nc.vector.tensor_tensor(self.k2[0:1, 146:147], self.k2[0:1, 146:147],
                                        self.maskc[0:1, 0:1], ALU.mult))
        self.vs(nc.vector.tensor_tensor(self.k2[0:1, 147:148], self.k2[0:1, 147:148],
                                        self.maskc[0:1, 1:2], ALU.mult))
        self.vs(nc.vector.tensor_copy(self.fs[0:1, 0:2], self.k2[0:1, 146:148]))
        self.t_use_bank(5)
        nc.tensor.wait_ge(self.sems["v"], self.cnt["v"])
        ins = nc.tensor.matmul(self.pb[5][:, 0:2], self.onesrowf[:, :], self.fs[0:1, 0:2],
                               start=True, stop=True)
        self.inc(ins, "t")
        self.wait(nc.vector, "t")
        nc.vector.tensor_scalar(self.sc2[:, 8:9], self.pb[5][:, 0:1], self.sc2[:, 0:1], None, ALU.mult)
        ins = nc.vector.tensor_scalar(self.sc2[:, 9:10], self.pb[5][:, 1:2], self.sc2[:, 0:1], None, ALU.mult)
        self.inc(ins, "v")
        self.consumed_bank(5, "v")

        for blk in range(NTB):
            self.wait(nc.vector, "t")
            for k in range(DCH):
                if blk == 0:
                    lsrc = self.k2[:, 112 + k:113 + k]
                else:
                    lsrc = self.bb[:, k * T + blk * 512 - 1: k * T + blk * 512]
                if blk == NTB - 1:
                    rsrc = self.k2[:, 120 + k:121 + k]
                else:
                    rsrc = self.bb[:, k * T + blk * 512 + 512: k * T + blk * 512 + 513]
                nc.vector.tensor_copy(self.k2[:, 128 + 2 * k:129 + 2 * k], lsrc)
                ins = nc.vector.tensor_copy(self.k2[:, 129 + 2 * k:130 + 2 * k], rsrc)
            self.inc(ins, "v")
            if blk == 0:
                lep = self.sc2[:, 8:9]
            else:
                lep = self.fs[:, 2048 + (blk - 1) * 512 + 511: 2048 + (blk - 1) * 512 + 512]
            if blk == NTB - 1:
                rep = self.sc2[:, 9:10]
            else:
                rep = self.fs[:, 2048 + (blk + 1) * 512: 2048 + (blk + 1) * 512 + 1]
            nc.vector.tensor_copy(self.fs[:, 1538:1539], lep)
            ins = nc.vector.tensor_copy(self.fs[:, 1539:1540], rep)
            self.inc(ins, "v")
            epil1 = self.fs[:, 2048 + blk * 512:2560 + blk * 512]

            for m0 in range(0, 32, 4):
                src = KQ.ap()[:, :].rearrange("p (k m) -> p k m", k=DCH)[:, :, m0 * 128:m0 * 128 + 512]
                self.wait(nc.sync, "t")
                self.dma(nc.sync, self.wg[:, 0:4096].rearrange("p (k m) -> p k m", k=DCH), src, "dW")
                self.wait(nc.tensor, "dW")
                for mi in range(4):
                    m = m0 + mi
                    bi = mi % 2
                    self.t_use_bank(bi, 4)
                    for k in range(DCH):
                        lhsT = self.wg[:, k * 512 + mi * 128: k * 512 + (mi + 1) * 128]
                        nc.tensor.matmul(self.pb[bi][:, 0:512], lhsT,
                                         self.bb[:, k * T + blk * 512: k * T + blk * 512 + 512],
                                         start=(k == 0), stop=(k == DCH - 1))
                        ins = nc.tensor.matmul(self.pb[4][:, 2 * bi:2 * bi + 2], lhsT,
                                               self.k2[:, 128 + 2 * k:130 + 2 * k],
                                               start=(k == 0), stop=(k == DCH - 1))
                    self.inc(ins, "t")
                    self.wait(nc.vector, "t")
                    h1f = self.fs[:, 0:512]
                    nc.vector.tensor_tensor(h1f, self.pb[bi][:, 0:512], epil1, ALU.mult)
                    hh1 = self.fs[:, 1536:1538]
                    ins = nc.vector.tensor_tensor(hh1, self.pb[4][:, 2 * bi:2 * bi + 2],
                                                  self.fs[:, 1538:1540], ALU.mult)
                    self.inc(ins, "v")
                    self.consumed_bank(bi, "v")
                    self.consumed_bank(4, "v")
                    wa = self.convw_sb[:, m * 3 + 0:m * 3 + 1]
                    wb = self.convw_sb[:, m * 3 + 1:m * 3 + 2]
                    wc = self.convw_sb[:, m * 3 + 2:m * 3 + 3]
                    t_ = self.fs[:, 1024 + 0:1024 + 512]
                    nc.vector.tensor_scalar(t_, h1f, wb, None, ALU.mult)
                    nc.vector.scalar_tensor_tensor(t_[:, 1:512], h1f[:, 0:511], wa, t_[:, 1:512],
                                                   ALU.mult, ALU.add)
                    nc.vector.scalar_tensor_tensor(t_[:, 0:1], hh1[:, 0:1], wa, t_[:, 0:1],
                                                   ALU.mult, ALU.add)
                    nc.vector.scalar_tensor_tensor(t_[:, 0:511], h1f[:, 1:512], wc, t_[:, 0:511],
                                                   ALU.mult, ALU.add)
                    ins = nc.vector.scalar_tensor_tensor(t_[:, 511:512], hh1[:, 1:2], wc, t_[:, 511:512],
                                                         ALU.mult, ALU.add)
                    self.inc(ins, "v")
                    self.wait(nc.scalar, "v")
                    sm = self.ab[:, m * 512:(m + 1) * 512]
                    ins = nc.scalar.activation(sm, t_, AF.Silu, bias=self.bcol("l0_convb", m), scale=1.0)
                    self.inc(ins, "s")
                    absb = self.cb[:, 0:512]
                    ins = nc.scalar.activation(absb, sm, AF.Abs)
                    self.inc(ins, "s")
                    self.wait(nc.vector, "s")
                    macc = self.fs[:, 512:1024]
                    if m == 0:
                        ins = nc.vector.tensor_copy(macc, absb)
                    else:
                        ins = nc.vector.tensor_max(macc, macc, absb)
                    self.inc(ins, "v")

            srow2 = self.fs[0:1, 0:512]
            for i4 in range(4):
                self.t_use_bank(5)
                nc.tensor.wait_ge(self.sems["v"], self.cnt["v"])
                ins = nc.tensor.transpose(self.pb[5][:, 0:128],
                                          self.fs[:, 512 + i4 * 128:512 + (i4 + 1) * 128], self.identf[:, :])
                self.inc(ins, "t")
                self.wait(nc.vector, "t")
                ins = nc.vector.reduce_max(self.fs[:, 1024 + i4:1025 + i4], self.pb[5][:, 0:128], axis=AX.X)
                self.inc(ins, "v")
                self.consumed_bank(5, "v")
            self.t_use_bank(5)
            nc.tensor.wait_ge(self.sems["v"], self.cnt["v"])
            for i4 in range(4):
                ins = nc.tensor.transpose(self.pb[5][0:1, i4 * 128:(i4 + 1) * 128],
                                          self.fs[:, 1024 + i4:1025 + i4], self.identf[:, :])
            self.inc(ins, "t")
            self.wait(nc.vector, "t")
            self.vs(nc.vector.tensor_scalar(srow2, self.pb[5][0:1, 0:512], 1e-5, None, ALU.max))
            self.consumed_bank(5, "v")
            scl2 = self.fs[0:1, 1024:1536]
            self.vs(nc.vector.reciprocal(scl2, srow2))
            self.vs(nc.vector.tensor_scalar(scl2, scl2, 127.0, None, ALU.mult))
            einv2 = self.fs[0:1, 512:1024]
            self.vs(nc.vector.tensor_scalar(einv2, srow2, 1.0 / 127.0, None, ALU.mult))
            self.t_use_bank(6, 7)
            nc.tensor.wait_ge(self.sems["v"], self.cnt["v"])
            nc.tensor.matmul(self.pb[6][:, 0:512], self.onesrowf[:, :], scl2, start=True, stop=True)
            ins = nc.tensor.matmul(self.pb[7][:, 0:512], self.onesrowf[:, :], einv2, start=True, stop=True)
            self.inc(ins, "t")
            self.wait(nc.vector, "t")
            epil2 = self.fs[:, 0:512]
            ins = nc.vector.tensor_scalar(epil2, self.pb[7][:, 0:512], self.sc2[:, 2:3], None, ALU.mult)
            self.inc(ins, "v")
            self.consumed_bank(7, "v")
            for m in range(32):
                sm = self.ab[:, m * 512:(m + 1) * 512]
                tmp = self.fs[:, 1024:1536]
                self.wait(nc.vector, "t")
                nc.vector.tensor_tensor(tmp, sm, self.pb[6][:, 0:512], ALU.mult)
                nc.vector.tensor_copy(self.ii[:, 0:512], tmp)
                ins = nc.vector.tensor_copy(sm, self.ii[:, 0:512])
                self.inc(ins, "v")
            self.consumed_bank(6, "v")

            for mo in range(DCH):
                src = self.w2q_hbm.ap()[:, :].rearrange("p (k m) -> p k m", k=32)[:, :, mo * 128:(mo + 1) * 128]
                self.wait(nc.sync, "t")
                self.wait(nc.sync, "v")
                self.dma(nc.sync, self.wg[:, 0:4096].rearrange("p (k m) -> p k m", k=32), src, "dW")
                self.wait(nc.tensor, "dW")
                bi = mo % 2
                self.t_use_bank(bi)
                for mk in range(32):
                    ins = nc.tensor.matmul(self.pb[bi][:, 0:512], self.wg[:, mk * 128:(mk + 1) * 128],
                                           self.ab[:, mk * 512:(mk + 1) * 512],
                                           start=(mk == 0), stop=(mk == 31))
                self.inc(ins, "t")
                self.wait(nc.vector, "t")
                ins = nc.vector.tensor_tensor(self.cb[:, mo * 512:(mo + 1) * 512],
                                              self.pb[bi][:, 0:512], epil2, ALU.mult)
                self.inc(ins, "v")
                self.consumed_bank(bi, "v")
            self.ln_pass(lambda c: self.cb[:, c * 512:(c + 1) * 512],
                         lambda c, sp: self.xs[:, c * T + blk * 512: c * T + blk * 512 + 512],
                         tlen=512,
                         residual=lambda c, sp: self.xs[:, c * T + blk * 512: c * T + blk * 512 + 512])
        if "x2" in self.dumps:
            self.dump_srcs["x2"] = self.xs[:, :]

    def p3_l1_attn(self):
        nc, din = self.nc, self.din
        KROW3, VROW3, TROW3 = 0, 2048, 4096
        for c in range(DCH):
            self.wait(nc.vector, "t")
            self.wait(nc.vector, "dC")
            ins = nc.vector.tensor_copy(self.bb[:, c * T:(c + 1) * T], self.xs[:, c * T:(c + 1) * T])
            self.inc(ins, "v")
        self.ln_pass(lambda c: self.bb[:, c * T:(c + 1) * T],
                     lambda c, sp: self.ab[:, c * T + sp * 512: c * T + sp * 512 + 512])

        bq8 = self.fs[:, 2048:2048 + 8]
        self.wait(nc.vector, "t")
        ins = nc.vector.tensor_scalar(bq8, self.biases[:, self.bidx["l1a_bq"]:self.bidx["l1a_bq"] + 8],
                                      0.125, None, ALU.mult)
        self.inc(ins, "v")

        stage = Pool(self, [self.cb[:, i * 512:(i + 1) * 512] for i in range(8)])

        def rhs_xl(k):
            return self.ab[:, k * T:(k + 1) * T]

        def evac_q(m, sp, ps, bi):
            self.wait(nc.vector, "t")
            si, st = stage.next(nc.vector)
            ins = nc.vector.tensor_scalar(st, ps, 0.125, bq8[:, m:m + 1], ALU.mult, ALU.add)
            self.inc(ins, "v")
            self.consumed_bank(bi, "v")
            self.wait(nc.gpsimd, "v")
            rowb = (m * 2 + sp // 2) * 128
            self.dma(nc.gpsimd, self.q_hbm.ap()[rowb:rowb + 128, (sp % 2) * 512:(sp % 2) * 512 + 512],
                     st, "dC")
            stage.mark(si)
        self.proj(din["l1a_Wq"], D, D, rhs_xl, evac_q)

        self.kn_ctr = 0
        def evac_k(m, sp, ps, bi):
            self.wait(nc.vector, "t")
            si, st = stage.next(nc.vector)
            ins = nc.vector.tensor_scalar(st, ps, self.bcol("l1a_bk", m), None, ALU.add)
            self.inc(ins, "v")
            self.consumed_bank(bi, "v")
            self.wait(nc.gpsimd, "v")
            kdst = self.cc3_in.ap()[KROW3 + sp * 512:KROW3 + (sp + 1) * 512, m * 128:(m + 1) * 128]
            kdst = kdst.rearrange("(t p) k -> p t k", p=128)
            self.dma(nc.gpsimd, kdst, st.rearrange("p (t k) -> p t k", k=128), "dC")
            stage.mark(si)
            sq = self.wg[:, 2048 + (self.kn_ctr % 2) * 512: 2048 + (self.kn_ctr % 2) * 512 + 512]
            self.wait(nc.vector, "t")
            ins = nc.vector.tensor_tensor(sq, st, st, ALU.mult)
            self.inc(ins, "v")
            self.t_use_bank(5)
            nc.tensor.wait_ge(self.sems["v"], self.cnt["v"])
            ins = nc.tensor.matmul(self.pb[5][0:65, 0:512], self.ind2[:, :], sq, start=True, stop=True)
            self.inc(ins, "t")
            self.wait(nc.vector, "t")
            ins = nc.vector.reduce_max(self.fs[0:65, 3584 + self.kn_ctr:3584 + self.kn_ctr + 1],
                                       self.pb[5][0:65, 0:512], axis=AX.X)
            self.inc(ins, "v")
            self.consumed_bank(5, "v")
            self.kn_ctr += 1
        self.proj(din["l1a_Wk"], D, D, rhs_xl, evac_k, mgrp=2)
        self.wait(nc.vector, "t")
        for c in range(DCH):
            ins = nc.vector.reduce_max(self.k2[0:65, c:c + 1],
                                       self.fs[0:65, 3584 + c * NTB:3584 + (c + 1) * NTB], axis=AX.X)
            self.inc(ins, "v")
        self.wait(nc.gpsimd, "v")
        self.dma(nc.gpsimd, self.cc3_in.ap()[TROW3:TROW3 + 65, 0:16], self.k2[0:65, 0:16], "dC")

        onecols = self.k2[:, 64:80]
        ins = nc.vector.memset(onecols, 1.0)
        self.inc(ins, "v")
        self.wait(nc.gpsimd, "v")
        with nc.allow_non_contiguous_dma("ones columns in Vt payload (tiny)"):
            for tt in range(16):
                dstv = self.cc3_in.ap()[VROW3 + tt * 128:VROW3 + (tt + 1) * 128, :]
                dstv = dstv.rearrange("p (h s) -> p h s", s=65)[:, :, 64:65]
                self.dma(nc.gpsimd, dstv, onecols.rearrange("p (h s) -> p h s", s=1), "dC")
        bvf = self.fs[0:1, 2048:2048 + 512]
        for halfc in range(2):
            self.wait(nc.sync, "t")
            self.dma(nc.sync, bvf, din["l1a_bv_row"].ap()[:, halfc * 512:halfc * 512 + 512], "dX")
            self.wait(nc.tensor, "dX")
            self.bcast_rows_to_fs(bvf, 2560 + halfc * 512, 512)
        for halfc in range(2):
            pieces = self.stream_w(din["l1a_Wv"], D, halfc * 512, 512)
            for tt in range(16):
                bi = tt % 2
                self.t_use_bank(bi)
                for k in range(DCH):
                    lhsT = self.ab[:, k * T + tt * 128: k * T + tt * 128 + 128]
                    ins = nc.tensor.matmul(self.pb[bi][:, 0:512], lhsT, pieces[k],
                                           start=(k == 0), stop=(k == DCH - 1))
                self.inc(ins, "t")
                self.wait(nc.vector, "t")
                si, st = stage.next(nc.vector)
                ins = nc.vector.tensor_tensor(st, self.pb[bi][:, 0:512],
                                              self.fs[:, 2560 + halfc * 512:2560 + halfc * 512 + 512],
                                              ALU.add)
                self.inc(ins, "v")
                self.consumed_bank(bi, "v")
                self.wait(nc.gpsimd, "v")
                dstv = self.cc3_in.ap()[VROW3 + tt * 128:VROW3 + (tt + 1) * 128, :]
                dstv = dstv.rearrange("p (h s) -> p h s", s=65)[:, halfc * 8:(halfc + 1) * 8, 0:64]
                self.dma(nc.gpsimd, dstv, st.rearrange("p (h s) -> p h s", s=64), "dC")
                stage.mark(si)

        g = nc.gpsimd
        self.wait(g, "dC")
        ins = g.collective_compute("AllGather", ALU.bypass, replica_groups=REPL,
                                   ins=[self.cc3_in.ap().opt()], outs=[self.cc3_out.ap().opt()])
        self.inc(ins, "CC")
        self.wait(g, "CC")
        CC3R = self.CC3R
        self.dma(g, self.k2[0:65, 16:32], self.cc3_out.ap()[TROW3:TROW3 + 65, 0:16], "dC")
        self.dma(g, self.k2[0:65, 32:48],
                 self.cc3_out.ap()[CC3R + TROW3:CC3R + TROW3 + 65, 0:16], "dC")
        self.wait(nc.vector, "dC")
        self.vs(nc.vector.tensor_tensor(self.rows[0:65, 0:8], self.k2[0:65, 16:24],
                                        self.k2[0:65, 32:40], ALU.max))
        self.wait(nc.scalar, "v")
        ins = nc.scalar.activation(self.rows[0:65, 0:8], self.rows[0:65, 0:8], AF.Sqrt)
        self.inc(ins, "s")

        pO = (self.pb[0], self.pb[1])
        pS_ = (self.pb[2], self.pb[3])
        qstage = self.bb[:, 0:4096]
        ptslots = [self.bb[:, 4608 + i * 512: 4608 + (i + 1) * 512] for i in range(4)]
        pt_i = 0
        for qb in range(NTB):
            self.wait(nc.sync, "v")
            self.wait(nc.sync, "t")
            u2, u1 = qb // 2, qb % 2
            for c in range(DCH):
                rowb = (c * 2 + u2) * 128
                self.dma(nc.sync, qstage[:, c * 512:(c + 1) * 512],
                         self.q_hbm.ap()[rowb:rowb + 128, u1 * 512:(u1 + 1) * 512], "dK")
            self.wait(nc.vector, "dK")
            for hp in range(DCH):
                sq = self.bb[:, 4096:4608]
                self.wait(nc.vector, "t")
                ins = nc.vector.tensor_tensor(sq, qstage[:, hp * 512:(hp + 1) * 512],
                                              qstage[:, hp * 512:(hp + 1) * 512], ALU.mult)
                self.inc(ins, "v")
                self.t_use_bank(5)
                nc.tensor.wait_ge(self.sems["v"], self.cnt["v"])
                ins = nc.tensor.matmul(self.pb[5][0:65, 0:512], self.ind2[:, :], sq, start=True, stop=True)
                self.inc(ins, "t")
                self.wait(nc.vector, "t")
                ins = nc.vector.reduce_max(self.rows[0:65, 24 + hp:25 + hp], self.pb[5][0:65, 0:512],
                                           axis=AX.X)
                self.inc(ins, "v")
                self.consumed_bank(5, "v")
            self.wait(nc.scalar, "v")
            ins = nc.scalar.activation(self.rows[0:65, 24:32], self.rows[0:65, 24:32], AF.Sqrt)
            self.inc(ins, "s")
            self.wait(nc.vector, "s")
            bv = self.rows[0:65, 32:40]
            self.vs(nc.vector.tensor_tensor(bv, self.rows[0:65, 24:32], self.rows[0:65, 0:8], ALU.mult))
            self.vs(nc.vector.tensor_scalar(bv, bv, -1.0, None, ALU.mult))
            self.t_use_bank(7)
            nc.tensor.wait_ge(self.sems["v"], self.cnt["v"])
            for j in range(16):
                rr, cc_ = (j % 2) * 64, j // 2
                ins = nc.tensor.matmul(self.pb[7][:, j:j + 1], self.onesall[rr:rr + 1, 0:128],
                                       bv[rr:rr + 1, cc_:cc_ + 1], start=True, stop=True)
            self.inc(ins, "t")
            self.wait(nc.vector, "t")
            self.wait(nc.vector, "s")
            ins = nc.vector.tensor_copy(self.fs[:, 3616:3632], self.pb[7][:, 0:16])
            self.inc(ins, "v")
            self.consumed_bank(7, "v")
            biascols = self.fs[:, 3616:3632]

            for half in range(4):
                self.wait(nc.sync, "t")
                self.wait(nc.gpsimd, "t")
                for jj in range(8):
                    t_g = half * 8 + jj
                    rk, tl = t_g // 16, t_g % 16
                    rowb = rk * CC3R + KROW3 + tl * 128
                    self.dma(nc.sync, self.ab[:, jj * 1024:(jj + 1) * 1024],
                             self.cc3_out.ap()[rowb:rowb + 128, 0:1024], "dK")
                    rowb = rk * CC3R + VROW3 + tl * 128
                    self.dma(nc.sync, self.cb[:, jj * 1040:(jj + 1) * 1040],
                             self.cc3_out.ap()[rowb:rowb + 128, 0:1040], "dK")
                self.wait(nc.tensor, "dK")
                for hp in range(DCH):
                    hA, hB = 2 * hp, 2 * hp + 1
                    for jj in range(8):
                        lA = self.ab[0:64, jj * 1024 + hp * 128: jj * 1024 + hp * 128 + 128]
                        lB = self.ab[64:128, jj * 1024 + hp * 128: jj * 1024 + hp * 128 + 128]
                        self.t_use_bank(2, 3)
                        nc.tensor.matmul(pS_[0][:, 0:512], lA, qstage[0:64, hp * 512:(hp + 1) * 512],
                                         start=True, stop=True)
                        ins = nc.tensor.matmul(pS_[1][:, 0:512], lB, qstage[64:128, hp * 512:(hp + 1) * 512],
                                               start=True, stop=True)
                        self.inc(ins, "t")
                        self.wait(nc.scalar, "t")
                        ptA = ptslots[pt_i % 4]; ptB = ptslots[(pt_i + 1) % 4]; pt_i += 2
                        nc.scalar.activation(ptA, pS_[0][:, 0:512], AF.Exp,
                                             bias=biascols[:, hA:hA + 1], scale=1.0)
                        ins = nc.scalar.activation(ptB, pS_[1][:, 0:512], AF.Exp,
                                                   bias=biascols[:, hB:hB + 1], scale=1.0)
                        self.inc(ins, "s")
                        self.consumed_bank(2, "s"); self.consumed_bank(3, "s")
                        if jj == 0:
                            self.t_use_bank(0, 1)
                        self.wait(nc.tensor, "s")
                        vA = self.cb[:, jj * 1040 + hA * 65: jj * 1040 + hA * 65 + 65]
                        vB = self.cb[:, jj * 1040 + hB * 65: jj * 1040 + hB * 65 + 65]
                        nc.tensor.matmul(pO[0][0:65, 0:512], vA, ptA, start=(jj == 0), stop=(jj == 7))
                        ins = nc.tensor.matmul(pO[1][0:65, 0:512], vB, ptB, start=(jj == 0), stop=(jj == 7))
                        self.inc(ins, "t")
                    self.wait(nc.vector, "t")
                    accA = self.bb[0:65, 8192 + hA * 512: 8192 + hA * 512 + 512]
                    accB = self.bb[0:65, 8192 + hB * 512: 8192 + hB * 512 + 512]
                    if half == 0:
                        nc.vector.tensor_copy(accA, pO[0][0:65, 0:512])
                        ins = nc.vector.tensor_copy(accB, pO[1][0:65, 0:512])
                    else:
                        nc.vector.tensor_add(accA, pO[0][0:65, 0:512], accA)
                        ins = nc.vector.tensor_add(accB, pO[1][0:65, 0:512], accB)
                    self.inc(ins, "v")
                    self.consumed_bank(0, "v"); self.consumed_bank(1, "v")
                    if half == 3:
                        for hh, acc in ((hA, accA), (hB, accB)):
                            comb = self.fs[0:65, 0:512]
                            ins = nc.vector.tensor_copy(comb, acc)
                            self.inc(ins, "v")
                            rec = self.fs[64:65, 0:512]
                            self.vs(nc.vector.reciprocal(rec, rec))
                            self.t_use_bank(6)
                            nc.tensor.wait_ge(self.sems["v"], self.cnt["v"])
                            ins = nc.tensor.matmul(self.pb[6][0:64, 0:512], self.onesall[64:65, 0:64],
                                                   rec, start=True, stop=True)
                            self.inc(ins, "t")
                            self.wait(nc.vector, "t")
                            ot = self.fs[0:64, 512:1024]
                            ins = nc.vector.tensor_tensor(ot, self.fs[0:64, 0:512],
                                                          self.pb[6][0:64, 0:512], ALU.mult)
                            self.inc(ins, "v")
                            self.consumed_bank(6, "v")
                            obf = self.bb[0:64, 4096:4608]
                            ins = nc.vector.tensor_copy(obf, ot)
                            self.inc(ins, "v")
                            self.wait(nc.gpsimd, "v")
                            rowb = (hh // 2) * 128 + (hh % 2) * 64
                            self.dma(nc.gpsimd,
                                     self.o_hbm.ap()[rowb:rowb + 64, qb * 512:(qb + 1) * 512], obf, "dC")

        self.wait(nc.sync, "t")
        self.wait(nc.sync, "dC")
        for c in range(DCH):
            self.dma(nc.sync, self.ab[:, c * T:(c + 1) * T],
                     self.o_hbm.ap()[c * 128:(c + 1) * 128, :], "dK")
        self.wait(nc.tensor, "dK")
        self.wait(nc.vector, "dK")

        def rhs_o(k):
            return self.ab[:, k * T:(k + 1) * T]

        def evac_wo(m, sp, ps, bi):
            self.wait(nc.vector, "t")
            dst = self.bb[:, m * T + sp * 512: m * T + sp * 512 + 512]
            ins = nc.vector.tensor_scalar(dst, ps, self.bcol("l1a_bo", m), None, ALU.add)
            self.inc(ins, "v")
            self.consumed_bank(bi, "v")

        self.proj(din["l1a_Wo"], D, D, rhs_o, evac_wo)
        self.ln_pass(lambda c: self.bb[:, c * T:(c + 1) * T],
                     lambda c, sp: self.xs[:, c * T + sp * 512: c * T + sp * 512 + 512],
                     residual=lambda c, sp: self.xs[:, c * T + sp * 512: c * T + sp * 512 + 512])
        if "x3" in self.dumps:
            self.dump_srcs["x3"] = self.xs[:, :]

    def p4_l1_ffn(self):
        nc, din = self.nc, self.din
        self.wait(nc.sync, "t")
        self.wait(nc.sync, "v")
        self.wait(nc.sync, "s")
        self.dma(nc.sync, self.x_spill.ap(), self.xs[:, :], "dC")
        self.wait(nc.vector, "dC")
        for c in range(DCH):
            ins = nc.vector.tensor_copy(self.bb[:, c * T:(c + 1) * T], self.xs[:, c * T:(c + 1) * T])
            self.inc(ins, "v")
        self.ln_pass(lambda c: self.bb[:, c * T:(c + 1) * T],
                     lambda c, sp: self.xs[:, c * T + sp * 512: c * T + sp * 512 + 512])
        if self.stop_after == "p4z":
            self.dump_srcs["x3"] = ("dram", self.x_spill.ap())
            return
        for l in (0, 1):
            if self.p4_layer(l):
                return
            if self.stop_after == f"p4l{l}":
                return
        self.wait(nc.vector, "t")
        for c in range(DCH):
            ins = nc.vector.tensor_copy(self.bb[:, c * T:(c + 1) * T], self.xs[:, c * T:(c + 1) * T])
            self.inc(ins, "v")
        outst = Pool(self, [self.fs[:, i * 512:(i + 1) * 512] for i in range(4)], "dOUT")
        src_bf = lambda c: self.bb[:, c * T:(c + 1) * T]
        pS, pQ, pM, pI = self.pb[4], self.pb[5], self.pb[6], self.pb[7]
        den = 1.0 / 1024.0
        for sp in range(NTB):
            sl = slice(sp * 512, sp * 512 + 512)
            self.t_use_bank(4, 5)
            for c in range(DCH):
                xb = src_bf(c)[:, sl]
                sq = self.wg[:, (c % 2) * 512:(c % 2) * 512 + 512]
                self.wait(nc.vector, "t")
                ins = nc.vector.tensor_tensor(sq, xb, xb, ALU.mult)
                self.inc(ins, "v")
                nc.tensor.wait_ge(self.sems["v"], self.cnt["v"])
                nc.tensor.matmul(pS[0:1, 0:512], self.onescol[:, :], xb,
                                 start=(c == 0), stop=(c == DCH - 1))
                ins = nc.tensor.matmul(pQ[0:1, 0:512], self.onescol[:, :], sq,
                                       start=(c == 0), stop=(c == DCH - 1))
                self.inc(ins, "t")
            self.wait(nc.vector, "t")
            r_mu = self.fs[0:1, 2048:2560]
            r_inv = self.fs[0:1, 2560:3072]
            r_mi = self.fs[0:1, 3072:3584]
            self.vs(nc.vector.tensor_scalar(r_mu, pS[0:1, 0:512], den, None, ALU.mult))
            self.vs(nc.vector.tensor_tensor(r_inv, r_mu, r_mu, ALU.mult))
            self.vs(nc.vector.tensor_scalar(r_inv, r_inv, float(EPS), None, ALU.subtract))
            self.vs(nc.vector.scalar_tensor_tensor(r_inv, pQ[0:1, 0:512], den, r_inv,
                                                   ALU.mult, ALU.subtract))
            self.consumed_bank(4, "v"); self.consumed_bank(5, "v")
            self.wait(nc.scalar, "v")
            ins = nc.scalar.activation(r_inv, r_inv, AF.Sqrt)
            self.inc(ins, "s")
            self.wait(nc.vector, "s")
            self.vs(nc.vector.reciprocal(r_inv, r_inv))
            self.vs(nc.vector.tensor_tensor(r_mi, r_mu, r_inv, ALU.mult))
            self.t_use_bank(6, 7)
            nc.tensor.wait_ge(self.sems["v"], self.cnt["v"])
            nc.tensor.matmul(pI[:, 0:512], self.onesrowf[:, :], r_inv, start=True, stop=True)
            ins = nc.tensor.matmul(pM[:, 0:512], self.onesrowf[:, :], r_mi, start=True, stop=True)
            self.inc(ins, "t")
            self.wait(nc.vector, "t")
            for c in range(DCH):
                self.wait(nc.sync, "v")
                self.dma(nc.sync, self.wf[:, 2048 + (c % 2) * 512:2048 + (c % 2) * 512 + 512],
                         self.x_spill.ap()[:, c * T + sp * 512: c * T + sp * 512 + 512], "dK")
                xb = src_bf(c)[:, sl]
                tmp = self.wf[:, (c % 2) * 512:(c % 2) * 512 + 512]
                self.wait(nc.vector, "dK")
                nc.vector.tensor_tensor(tmp, xb, pI[:, 0:512], ALU.mult)
                nc.vector.tensor_tensor(tmp, tmp, pM[:, 0:512], ALU.subtract)
                si, st = outst.next(nc.vector)
                ins = nc.vector.tensor_add(st, self.wf[:, 2048 + (c % 2) * 512:2048 + (c % 2) * 512 + 512],
                                           tmp)
                self.inc(ins, "v")
                self.wait(nc.gpsimd, "v")
                ins = nc.gpsimd.dma_start(out=self.out.ap()[c * 128:(c + 1) * 128, sp * 512:(sp + 1) * 512],
                                          in_=st)
                self.inc(ins, "dOUT", 16)
                outst.mark(si)
            self.consumed_bank(6, "v"); self.consumed_bank(7, "v")
        self.out_written = True
        nc.gpsimd.wait_ge(self.sems["dOUT"], self.cnt["dOUT"])

    def p4_layer(self, l):
        nc, din = self.nc, self.din
        cc_in = self.cc4_in if l == 0 else self.cc5_in
        cc_out = self.cc4_out if l == 0 else self.cc5_out
        def zsrc(c, sp=None):
            if c < 4:
                base = self.ab[:, c * T:(c + 1) * T]
            else:
                base = self.cb[:, (c - 4) * 2049 + 1:(c - 4) * 2049 + 1 + T]
            return base
        def zshift(c):
            assert c >= 4
            return self.cb[:, (c - 4) * 2049:(c - 4) * 2049 + T]
        for c in range(DCH):
            self.wait(nc.vector, "t")
            self.wait(nc.vector, "dC")
            ins = nc.vector.tensor_copy(zsrc(c), self.xs[:, c * T:(c + 1) * T])
            self.inc(ins, "v")
        self.wait(nc.gpsimd, "v")
        for c in range(DCH):
            self.dma(nc.gpsimd, cc_in.ap()[:, c * 513:(c + 1) * 513], zsrc(c)[:, T - 513:T], "dC")
        g = nc.gpsimd
        self.wait(g, "dC")
        ins = g.collective_compute("AllGather", ALU.bypass, replica_groups=REPL,
                                   ins=[cc_in.ap().opt()], outs=[cc_out.ap().opt()])
        self.inc(ins, "CC")
        self.wait(g, "CC")
        recv_of = lambda c: self.fsb[:, 3584 + c * 513:3584 + (c + 1) * 513]
        for c in range(DCH):
            self.dma(g, recv_of(c), cc_out.ap()[0:128, c * 513:(c + 1) * 513], "dC")
        self.wait(nc.vector, "dC")
        for c in range(4, DCH):
            ins = nc.vector.tensor_scalar(self.cb[:, (c - 4) * 2049:(c - 4) * 2049 + 1],
                                          recv_of(c)[:, 512:513],
                                          self.maskc[:, 0:1], None, ALU.mult)
            self.inc(ins, "v")

        wq = din[f"l1f_wqkv{l}"]

        def rhs_sh(k):
            return zsrc(k) if k < 4 else zshift(k)

        def evac_qk(m, sp, ps, bi):
            self.wait(nc.vector, "t")
            dst = self.bb[:, m * T + sp * 512: m * T + sp * 512 + 512]
            ins = nc.vector.tensor_copy(dst, ps)
            self.inc(ins, "v")
            self.consumed_bank(bi, "v")
        self.proj(wq, D, 2 * INNER, rhs_sh, evac_qk)
        vst = self.fsb[:, 0:520]
        self.wait(nc.vector, "dOUT")
        ins = nc.vector.memset(vst, 1.0)
        self.inc(ins, "v")
        pieces = self.stream_w(wq, D, 2 * INNER, 512)
        for tt in range(16):
            bi = tt % 2
            self.t_use_bank(bi)
            for k in range(DCH):
                lhsT = rhs_sh(k)[:, tt * 128:(tt + 1) * 128]
                ins = nc.tensor.matmul(self.pb[bi][:, 0:512], lhsT, pieces[k],
                                       start=(k == 0), stop=(k == DCH - 1))
            self.inc(ins, "t")
            self.wait(nc.vector, "t")
            self.wait(nc.vector, "dC")
            dst3 = vst.rearrange("p (h s) -> p h s", s=65)[:, :, 0:64]
            ins = nc.vector.tensor_copy(dst3, self.pb[bi][:, 0:512].rearrange("p (h s) -> p h s", s=64))
            self.inc(ins, "v")
            self.consumed_bank(bi, "v")
            self.wait(nc.gpsimd, "v")
            self.dma(nc.gpsimd, self.vt_hbm.ap()[tt * 128:(tt + 1) * 128, :], vst, "dC")

        def rhs_bnd(k):
            base = recv_of(k)
            return base[:, 1:513] if k < 4 else base[:, 0:512]
        kb_off = 4680
        pieces = self.stream_w(wq, D, INNER, 512)
        for m in range(4):
            bi = m % 2
            self.t_use_bank(bi)
            for k in range(DCH):
                lhsT = pieces[k][:, m * 128:(m + 1) * 128]
                ins = nc.tensor.matmul(self.pb[bi][:, 0:512], lhsT, rhs_bnd(k), start=(k == 0),
                                       stop=(k == DCH - 1))
            self.inc(ins, "t")
            self.wait(nc.vector, "t")
            ins = nc.vector.tensor_copy(self.cb[:, kb_off + m * 512:kb_off + (m + 1) * 512],
                                        self.pb[bi][:, 0:512])
            self.inc(ins, "v")
            self.consumed_bank(bi, "v")
        pieces = self.stream_w(wq, D, 2 * INNER, 512)
        vst2 = self.fsb[:, 0:520]
        for tt in range(4):
            bi = tt % 2
            self.t_use_bank(bi)
            for k in range(DCH):
                lhsT = rhs_bnd(k)[:, tt * 128:(tt + 1) * 128]
                ins = nc.tensor.matmul(self.pb[bi][:, 0:512], lhsT, pieces[k],
                                       start=(k == 0), stop=(k == DCH - 1))
            self.inc(ins, "t")
            self.wait(nc.vector, "t")
            self.wait(nc.vector, "dC")
            dst3 = vst2.rearrange("p (h s) -> p h s", s=65)[:, :, 0:64]
            ins = nc.vector.tensor_copy(dst3, self.pb[bi][:, 0:512].rearrange("p (h s) -> p h s", s=64))
            self.inc(ins, "v")
            self.consumed_bank(bi, "v")
            self.wait(nc.gpsimd, "v")
            self.dma(nc.gpsimd, self.vt_hbm.ap()[(16 + tt) * 128:(17 + tt) * 128, :], vst2, "dC")

        self.p4_rotary(l, lambda c: self.bb[:, c * T:(c + 1) * T], T, din["rot_own"], 0, is_q=True)
        self.p4_rotary(l, lambda c: self.bb[:, (4 + c) * T:(5 + c) * T], T, din["rot_own"], 0, is_q=False)
        self.p4_rotary(l, lambda c: self.cb[:, kb_off + c * 512:kb_off + (c + 1) * 512], 512,
                       din["rot_bnd"], 1, is_q=False)
        if self.stop_after == f"p4qk{l}":
            self.dump_srcs["qk"] = self.bb[:, :]
            self.dump_srcs["kb"] = self.cb[:, :]
            return True

        self.p4_local(l)
        if self.stop_after == f"p4o{l}":
            self.dump_srcs["o"] = self.ab[:, :]
            return True

        def rhs_o(k):
            return self.ab[:, k * T:(k + 1) * T]

        def evac_wo(m, sp, ps, bi):
            self.wait(nc.vector, "t")
            dst = self.xs[:, m * T + sp * 512: m * T + sp * 512 + 512]
            ins = nc.vector.tensor_add(dst, dst, ps)
            self.inc(ins, "v")
            self.consumed_bank(bi, "v")
        self.proj(din[f"l1f_wo{l}"], INNER, D, rhs_o, evac_wo)
        if self.stop_after == f"p4w{l}":
            return True

        self.wait(nc.vector, "t")
        for c in range(DCH):
            ins = nc.vector.tensor_copy(self.k2[:, 176 + c:177 + c], self.xs[:, c * T + T - 1:c * T + T])
            self.inc(ins, "v")
        self.wait(nc.gpsimd, "v")
        self.dma(nc.gpsimd, self.cc2b_in.ap()[:, 0:8], self.k2[:, 176:184], "dC")
        g = nc.gpsimd
        self.wait(g, "dC")
        ins = g.collective_compute("AllGather", ALU.bypass, replica_groups=REPL,
                                   ins=[self.cc2b_in.ap().opt()], outs=[self.cc2b_out.ap().opt()])
        self.inc(ins, "CC")
        self.wait(g, "CC")
        self.dma(g, self.k2[:, 184:192], self.cc2b_out.ap()[0:128, 0:8], "dC")

        for c in range(DCH):
            self.wait(nc.vector, "t")
            self.wait(nc.vector, "dC")
            ins = nc.vector.tensor_copy(zsrc(c), self.xs[:, c * T:(c + 1) * T])
            self.inc(ins, "v")
        self.wait(nc.vector, "dC")
        for c in range(4, DCH):
            ins = nc.vector.tensor_scalar(self.cb[:, (c - 4) * 2049:(c - 4) * 2049 + 1],
                                          self.k2[:, 184 + c:185 + c],
                                          self.maskc[:, 0:1], None, ALU.mult)
            self.inc(ins, "v")
        gb = lambda c: (self.bcol(f"l1f_lng{l}", c), self.bcol(f"l1f_lnb{l}", c))
        self.ln_pass(lambda c: rhs_sh(c),
                     lambda c, sp: self.bb[:, c * T + sp * 512: c * T + sp * 512 + 512],
                     gamma_beta=gb)
        if self.stop_after == f"p4h{l}":
            self.dump_srcs["h2ln"] = self.bb[:, :]
            return True

        def hslot_of(si):
            c0 = si * 2048 if si < 8 else 32768 + (si - 8) * 2048
            return self.hg[:, c0:c0 + 2048]
        grp_slots = [list(range(14)), list(range(14, 22))]
        for gi, slots in enumerate(grp_slots):
            for m in slots:
                si = slots.index(m)
                hslot = hslot_of(si)
                srcA = din[f"l1f_w1a{l}"].ap()[:, m * 128:(m + 1) * 128].rearrange("(k p) m -> p k m", p=128)
                srcG = din[f"l1f_w1g{l}"].ap()[:, m * 128:(m + 1) * 128].rearrange("(k p) m -> p k m", p=128)
                self.wait(nc.sync, "v")
                self.dma(nc.sync, self.wf[:, 0:1024].rearrange("p (k m) -> p k m", k=DCH), srcA, "dW")
                self.dma(nc.sync, self.wf[:, 1024:2048].rearrange("p (k m) -> p k m", k=DCH), srcG, "dW")
                self.wait(nc.vector, "dW")
                self.wait(nc.vector, "t")
                ins = nc.vector.tensor_copy(self.wg[:, 0:2048], self.wf[:, 0:2048])
                self.inc(ins, "v")
                nc.tensor.wait_ge(self.sems["v"], self.cnt["v"])
                self.t_use_bank(0, 1, 2, 3, 4, 5, 6, 7)
                for sp in range(NTB):
                    for k in range(DCH):
                        lhsA = self.wg[:, k * 128:(k + 1) * 128]
                        lhsG = self.wg[:, 1024 + k * 128:1024 + (k + 1) * 128]
                        rhs = self.bb[:, k * T + sp * 512: k * T + sp * 512 + 512]
                        nc.tensor.matmul(self.pb[sp][:, 0:512], lhsA, rhs,
                                         start=(k == 0), stop=(k == DCH - 1))
                        ins = nc.tensor.matmul(self.pb[4 + sp][:, 0:512], lhsG, rhs,
                                               start=(k == 0), stop=(k == DCH - 1))
                    self.inc(ins, "t")
                for sp in range(NTB):
                    self.wait(nc.scalar, "t")
                    self.wait(nc.scalar, "v")
                    gt = self.wg[:, 2048:2560]
                    ins = nc.scalar.activation(gt, self.pb[4 + sp][:, 0:512], AF.Gelu,
                                               bias=self.bcol(f"l1f_b1g{l}", m), scale=1.0)
                    self.inc(ins, "s")
                    self.wait(nc.vector, "s")
                    ins = nc.vector.scalar_tensor_tensor(hslot[:, sp * 512:(sp + 1) * 512],
                                                         self.pb[sp][:, 0:512],
                                                         self.bcol(f"l1f_b1a{l}", m), gt,
                                                         ALU.add, ALU.mult)
                    self.inc(ins, "v")
                    self.consumed_bank(sp, "v"); self.consumed_bank(4 + sp, "v")
            for mo in range(DCH):
                src = din[f"l1f_w2{l}"].ap()[slots[0] * 128:(slots[-1] + 1) * 128, mo * 128:(mo + 1) * 128]
                src = src.rearrange("(k p) m -> p k m", p=128)
                nmk = len(slots)
                self.wait(nc.sync, "v")
                self.dma(nc.sync, self.wf[:, 0:nmk * 128].rearrange("p (k m) -> p k m", k=nmk), src, "dW")
                self.wait(nc.vector, "dW")
                self.wait(nc.vector, "t")
                ins = nc.vector.tensor_copy(self.wg[:, 0:nmk * 128], self.wf[:, 0:nmk * 128])
                self.inc(ins, "v")
                nc.tensor.wait_ge(self.sems["v"], self.cnt["v"])
                for sp in range(NTB):
                    bi = sp % 2
                    self.t_use_bank(bi)
                    for ki, m in enumerate(slots):
                        hslot = hslot_of(ki)
                        ins = nc.tensor.matmul(self.pb[bi][:, 0:512], self.wg[:, ki * 128:(ki + 1) * 128],
                                               hslot[:, sp * 512:(sp + 1) * 512],
                                               start=(ki == 0), stop=(ki == nmk - 1))
                    self.inc(ins, "t")
                    self.wait(nc.vector, "t")
                    dst = self.xs[:, mo * T + sp * 512: mo * T + sp * 512 + 512]
                    if gi == 0:
                        ins = nc.vector.scalar_tensor_tensor(dst, self.pb[bi][:, 0:512],
                                                             self.bcol(f"l1f_b2{l}", mo), dst,
                                                             ALU.add, ALU.add)
                    else:
                        ins = nc.vector.tensor_add(dst, dst, self.pb[bi][:, 0:512])
                    self.inc(ins, "v")
                    self.consumed_bank(bi, "v")
        return False

    def p4_rotary(self, l, chunk_of, tlen, rotdram, tbl_off, is_q):
        nc = self.nc
        scol = self.bcol(f"l1f_qs{l}" if is_q else f"l1f_ks{l}")
        nsp = tlen // 512
        sc = self.sc2[:, 10:11]
        self.wait(nc.vector, "t")
        self.vs(nc.vector.tensor_scalar(sc, scol, 8.0 if is_q else 1.0, None, ALU.mult))
        for sp in range(nsp):
            base = (0 if is_q else 2) * 128
            c0 = tbl_off + sp * 512
            self.wait(nc.sync, "v")
            self.dma(nc.sync, self.fs[:, 2048:2560], rotdram.ap()[base:base + 128, c0:c0 + 512], "dK")
            self.dma(nc.sync, self.fs[:, 2560:3072],
                     rotdram.ap()[base + 128:base + 256, c0:c0 + 512], "dK")
            self.wait(nc.vector, "dK")
            for c in range(4):
                ch = chunk_of(c)[:, sp * 512:(sp + 1) * 512]
                sq = self.wg[:, 0:512]
                self.wait(nc.vector, "t")
                ins = nc.vector.tensor_tensor(sq, ch, ch, ALU.mult)
                self.inc(ins, "v")
                self.t_use_bank(5)
                nc.tensor.wait_ge(self.sems["v"], self.cnt["v"])
                ins = nc.tensor.matmul(self.pb[5][0:65, 0:512], self.ind2[:, :], sq, start=True, stop=True)
                self.inc(ins, "t")
                self.wait(nc.vector, "t")
                rq = self.fs[0:65, 3072:3584]
                self.vs(nc.vector.tensor_scalar(rq, self.pb[5][0:65, 0:512], 1e-24, None, ALU.max))
                self.consumed_bank(5, "v")
                self.wait(nc.scalar, "v")
                ins = nc.scalar.activation(rq, rq, AF.Sqrt)
                self.inc(ins, "s")
                self.wait(nc.vector, "s")
                self.vs(nc.vector.reciprocal(rq, rq))
                self.t_use_bank(6)
                nc.tensor.wait_ge(self.sems["v"], self.cnt["v"])
                nc.tensor.matmul(self.pb[6][0:64, 0:512], self.onesall[0:1, 0:64], rq[0:1, :],
                                 start=True, stop=True)
                ins = nc.tensor.matmul(self.pb[6][64:128, 0:512], self.onesall[64:65, 0:64],
                                       rq[64:65, :], start=True, stop=True)
                self.inc(ins, "t")
                self.wait(nc.vector, "t")
                t1 = self.wf[:, 0:512]
                nc.vector.tensor_tensor(t1, ch, self.pb[6][:, 0:512], ALU.mult)
                ins = nc.vector.tensor_scalar(ch, t1, sc, None, ALU.mult)
                self.inc(ins, "v")
                self.consumed_bank(6, "v")
                self.t_use_bank(5)
                nc.tensor.wait_ge(self.sems["v"], self.cnt["v"])
                ins = nc.tensor.matmul(self.pb[5][:, 0:512], self.perm[:, :], ch, start=True, stop=True)
                self.inc(ins, "t")
                self.wait(nc.vector, "t")
                u = self.wf[:, 0:512]
                v_ = self.wf[:, 512:1024]
                nc.vector.tensor_tensor(u, ch, self.fs[:, 2048:2560], ALU.mult)
                nc.vector.tensor_tensor(v_, self.pb[5][:, 0:512], self.fs[:, 2560:3072], ALU.mult)
                ins = nc.vector.tensor_add(ch, u, v_)
                self.inc(ins, "v")
                self.consumed_bank(5, "v")

    def p4_local(self, l):
        nc = self.nc
        self.wait(nc.sync, "t")
        self.wait(nc.sync, "v")
        self.dma(nc.sync, self.cb[:, 8192:12288], self.din["l1f_maskw"].ap(), "dK")
        self.wait(nc.vector, "dK")
        qscol, kscol = self.bcol(f"l1f_qs{l}"), self.bcol(f"l1f_ks{l}")
        self.wait(nc.scalar, "v")
        self.wait(nc.scalar, "t")
        ins = nc.scalar.activation(self.rows[:, 44:45], qscol, AF.Abs)
        self.inc(ins, "s")
        ins = nc.scalar.activation(self.rows[:, 45:46], kscol, AF.Abs)
        self.inc(ins, "s")
        self.t_use_bank(5)
        nc.tensor.wait_ge(self.sems["s"], self.cnt["s"])
        nc.tensor.matmul(self.pb[5][0:1, 0:128], self.rows[:, 44:45], self.identf[:, :],
                         start=True, stop=True)
        ins = nc.tensor.matmul(self.pb[5][0:1, 128:256], self.rows[:, 45:46], self.identf[:, :],
                               start=True, stop=True)
        self.inc(ins, "t")
        self.wait(nc.vector, "t")
        mq, mk = self.rows[0:1, 46:47], self.rows[0:1, 47:48]
        self.vs(nc.vector.reduce_max(mq, self.pb[5][0:1, 0:128], axis=AX.X))
        self.vs(nc.vector.reduce_max(mk, self.pb[5][0:1, 128:256], axis=AX.X))
        self.consumed_bank(5, "v")
        negm = self.rows[0:1, 48:49]
        self.vs(nc.vector.tensor_tensor(negm, mq, mk, ALU.mult))
        self.vs(nc.vector.tensor_scalar(negm, negm, -8.4, -4.0, ALU.mult, ALU.add))
        self.t_use_bank(7)
        nc.tensor.wait_ge(self.sems["v"], self.cnt["v"])
        ins = nc.tensor.matmul(self.pb[7][:, 0:1], self.onesall[0:1, 0:128], negm,
                               start=True, stop=True)
        self.inc(ins, "t")
        self.wait(nc.vector, "t")
        biascols = self.fs[:, 3616:3618]
        ins = nc.vector.tensor_copy(biascols[:, 0:1], self.pb[7][:, 0:1])
        self.inc(ins, "v")
        self.consumed_bank(7, "v")
        mneg = self.fs[:, 3618:3619]
        self.vs(nc.vector.tensor_scalar(mneg, self.maskc[:, 2:3], 1e30, -1e30, ALU.mult, ALU.add))
        self.vs(nc.vector.tensor_tensor(biascols[:, 1:2], biascols[:, 0:1], mneg, ALU.add))

        pO = (self.pb[0], self.pb[1])
        pS_ = (self.pb[2], self.pb[3])
        for w in range(NTB):
            self.wait(nc.sync, "t")
            for kt in range(8):
                if w == 0 and kt < 4:
                    tg = 16 + kt
                else:
                    tg = (w - 1) * 4 + kt
                self.dma(nc.sync, self.cb[:, kt * 520:(kt + 1) * 520],
                         self.vt_hbm.ap()[tg * 128:(tg + 1) * 128, :], "dK")
            self.wait(nc.tensor, "dK")
            for hp in range(4):
                hA, hB = 2 * hp, 2 * hp + 1
                for kt in range(8):
                    if kt < 4:
                        if w == 0:
                            kap = self.cb[:, 4680 + hp * 512 + kt * 128:4680 + hp * 512 + kt * 128 + 128]
                        else:
                            kap = self.bb[:, (4 + hp) * T + (w - 1) * 512 + kt * 128:
                                          (4 + hp) * T + (w - 1) * 512 + kt * 128 + 128]
                    else:
                        kap = self.bb[:, (4 + hp) * T + w * 512 + (kt - 4) * 128:
                                      (4 + hp) * T + w * 512 + (kt - 4) * 128 + 128]
                    qap = self.bb[:, hp * T + w * 512: hp * T + w * 512 + 512]
                    self.t_use_bank(2, 3)
                    nc.tensor.matmul(pS_[0][:, 0:512], kap[0:64, :], qap[0:64, :], start=True, stop=True)
                    ins = nc.tensor.matmul(pS_[1][:, 0:512], kap[64:128, :], qap[64:128, :],
                                           start=True, stop=True)
                    self.inc(ins, "t")
                    self.wait(nc.vector, "t")
                    mw = self.cb[:, 8192 + kt * 512:8192 + (kt + 1) * 512]
                    for bi2 in (0, 1):
                        ins = nc.vector.tensor_tensor(pS_[bi2][:, 0:512], pS_[bi2][:, 0:512],
                                                      mw, ALU.add)
                    self.inc(ins, "v")
                    nc.scalar.wait_ge(self.sems["v"], self.cnt["v"])
                    self.wait(nc.scalar, "t")
                    bsel = 1 if (w == 0 and kt < 4) else 0
                    ptA = self.cb[:, 6912:7424]
                    ptB = self.cb[:, 7424:7936]
                    nc.scalar.activation(ptA, pS_[0][:, 0:512], AF.Exp,
                                         bias=biascols[:, bsel:bsel + 1], scale=1.0)
                    ins = nc.scalar.activation(ptB, pS_[1][:, 0:512], AF.Exp,
                                               bias=biascols[:, bsel:bsel + 1], scale=1.0)
                    self.inc(ins, "s")
                    self.consumed_bank(2, "s"); self.consumed_bank(3, "s")
                    if kt == 0:
                        self.t_use_bank(0, 1)
                    self.wait(nc.tensor, "s")
                    vA = self.cb[:, kt * 520 + hA * 65: kt * 520 + hA * 65 + 65]
                    vB = self.cb[:, kt * 520 + hB * 65: kt * 520 + hB * 65 + 65]
                    nc.tensor.matmul(pO[0][0:65, 0:512], vA, ptA, start=(kt == 0), stop=(kt == 7))
                    ins = nc.tensor.matmul(pO[1][0:65, 0:512], vB, ptB, start=(kt == 0), stop=(kt == 7))
                    self.inc(ins, "t")
                for bi2, hh in ((0, hA), (1, hB)):
                    self.wait(nc.vector, "t")
                    comb = self.fs[0:65, 0:512]
                    ins = nc.vector.tensor_copy(comb, pO[bi2][0:65, 0:512])
                    self.inc(ins, "v")
                    self.consumed_bank(bi2, "v")
                    rec = self.fs[64:65, 0:512]
                    self.vs(nc.vector.reciprocal(rec, rec))
                    self.t_use_bank(6)
                    nc.tensor.wait_ge(self.sems["v"], self.cnt["v"])
                    ins = nc.tensor.matmul(self.pb[6][0:64, 0:512], self.onesall[64:65, 0:64], rec,
                                           start=True, stop=True)
                    self.inc(ins, "t")
                    self.wait(nc.vector, "t")
                    c, rr = hh // 2, (hh % 2) * 64
                    dst = self.ab[rr:rr + 64, c * T + w * 512: c * T + w * 512 + 512]
                    ins = nc.vector.tensor_tensor(dst, self.fs[0:64, 0:512], self.pb[6][0:64, 0:512],
                                                  ALU.mult)
                    self.inc(ins, "v")
                    self.consumed_bank(6, "v")

    def _finish(self):
        nc = self.nc
        nc.all_engine_barrier()
        st = self.sem("dOUT")
        for name in self.dumps:
            src = self.dump_srcs.get(name)
            if src is None:
                continue
            if isinstance(src, tuple) and src[0] == "dram":
                src = src[1]
            ins = nc.gpsimd.dma_start(out=self.dbg[name].ap(), in_=src)
            ins.then_inc(st, 16); self.cnt["dOUT"] += 16
        if not self.out_written:
            for c in range(DCH):
                ins = nc.gpsimd.dma_start(out=self.out.ap()[c * 128:(c + 1) * 128, :],
                                          in_=self.xs[:, c * T:(c + 1) * T])
                ins.then_inc(st, 16); self.cnt["dOUT"] += 16
        nc.gpsimd.wait_ge(st, self.cnt["dOUT"])



_BUILD_CACHE = {}


def _run(inputs, stop_after=None, dumps=(), want_cc=()):
    g, cores, bidx, nb = prepare_host(inputs)
    bld = Builder(bidx, nb, stop_after=stop_after, dumps=dumps)
    nc = bld.build()
    in_maps = []
    for cid in range(NCORES):
        m = dict(g)
        m.update(cores[cid])
        in_maps.append(m)
    res = run_bass_kernel_spmd(nc, in_maps, core_ids=list(range(NCORES)))
    return bld, res


def kernel(**inputs):
    bld, res = _run(inputs)
    out = np.zeros((B, S, D), np.float32)
    for cid in range(NCORES):
        b_, hf = divmod(cid, 2)
        out[b_, hf * T:(hf + 1) * T, :] = res.results[cid]["out"].T
    return out
